# version 10
# baseline (speedup 1.0000x reference)
import contextlib
import math

import numpy as np
import ml_dtypes

import concourse.bass as bass
import concourse.mybir as mybir
from concourse.bass_utils import run_bass_kernel_spmd

F32 = mybir.dt.float32
BF16 = mybir.dt.bfloat16
AF = mybir.ActivationFunctionType
ALU = mybir.AluOpType
AX = mybir.AxisListType

HID = 4096
S = 2048
NQ, NKV, D = 32, 8, 128
INT = 8192
LAT = 2048
HC = HID // 128
IC = INT // 128
LC = LAT // 128
EPS = 1e-5
QSCALE = 1.0 / math.sqrt(D)
TA = 512
NTA = S // TA
TB = 1024
NTB = S // TB

ENGINES = ("tensor", "vector", "scalar", "gpsimd", "sync")

SEM_NAMES = [
    "BAR", "P0", "DVE_CAST", "PE_STATF", "DVE_ROWS", "ACT_ROWS", "PE_BC",
    "BC_FREE", "DVE_Y", "PE_G", "DVE_EVAC", "ACT_EVAC", "DVE_H", "DVE_SBF",
    "PE_SC", "DVE_MAX", "ACT_P", "DVE_RCP", "GP_PN", "PE_PT2", "PE_CTX2",
    "DVE_CTX", "CTX_ST", "CTL_LD", "CTL_FREE", "P0G",
]


class Plan:

    def __init__(self, nc, es):
        self.nc = nc
        self.q = {e: [] for e in ENGINES}
        self.sems = {}
        self.cnt = {}
        self.es = es
        self.dma_marks = set()
        self.fence_sb = None
        self.nbar = 0

    def sem(self, name):
        if name not in self.sems:
            self.sems[name] = self.es.enter_context(self.nc.semaphore(name))
            self.cnt[name] = 0
        return self.sems[name]

    def bump(self, name, n=1):
        self.sem(name)
        self.cnt[name] += n
        return self.cnt[name]

    def val(self, name):
        self.sem(name)
        return self.cnt[name]

    def add(self, eng, fn):
        self.q[eng].append(fn)

    def flush(self):
        nc = self.nc
        with nc.Block() as blk:
            for e in ENGINES:
                fns = self.q[e]
                if not fns:
                    continue
                getattr(blk, e)(lambda eng, fns=fns: [f(eng) for f in fns])
        self.q = {e: [] for e in ENGINES}

    def barrier(self):
        waits = [(n, self.cnt[n]) for n in sorted(self.dma_marks)
                 if self.cnt[n] > 0]

        def fw(eng, waits=tuple(waits)):
            for n, v in waits:
                eng.wait_ge(self.sems[n], v)
        self.add("sync", fw)
        self.nbar += 1
        bar = self.sem("BAR")
        tgt = self.bump("BAR", len(ENGINES))
        for e in ENGINES:
            def f(eng, bar=bar, tgt=tgt, e=e):
                if e in ("vector", "gpsimd"):
                    eng.engine_nop().then_inc(bar, 1)
                elif e == "scalar" and self.fence_sb is not None and self.nbar > 1:
                    eng.mul(self.fence_sb[0:1, 0:1], self.fence_sb[0:1, 0:1],
                            1.0).then_inc(bar, 1)
                else:
                    eng.nop().then_inc(bar, 1)
                eng.wait_ge(bar, tgt)
            self.add(e, f)


class Ring:

    def __init__(self, p, name, slots, queues=("sync",)):
        self.p = p
        self.slots = slots
        self.n = len(slots)
        self.i = 0
        self.ld = {}
        self.ld_n = {}
        for q in queues:
            if q == "eng":
                self.ld[q] = [p.sem(f"{name}_{q}_ld")] * self.n
                self.ld_n[q] = [f"{name}_{q}_ld"] * self.n
            else:
                self.ld[q] = [p.sem(f"{name}_{q}_ld{i}") for i in range(self.n)]
                self.ld_n[q] = [f"{name}_{q}_ld{i}" for i in range(self.n)]
        self.fr = [p.sem(f"{name}_fr{i}") for i in range(self.n)]
        self.fr_n = [f"{name}_fr{i}" for i in range(self.n)]
        self.fr_expect = [0] * self.n

    def produce(self, queue, dma_fn, free_amt, n_dma=1, extra_waits=()):
        p = self.p
        slot = self.i
        self.i = (self.i + 1) % self.n
        need_free = self.fr_expect[slot]
        buf = self.slots[slot]
        ld_sem, fr_sem = self.ld[queue][slot], self.fr[slot]

        def f(eng, buf=buf, need_free=need_free, dma_fn=dma_fn, fr_sem=fr_sem,
              ld_sem=ld_sem, extra_waits=tuple(extra_waits)):
            for name, v in extra_waits:
                if v:
                    eng.wait_ge(p.sems[name], v)
            if need_free:
                eng.wait_ge(fr_sem, need_free)
            instrs = dma_fn(eng, buf)
            if not isinstance(instrs, (list, tuple)):
                instrs = [instrs]
            for i in instrs:
                i.then_inc(ld_sem, 16)
        p.add(queue, f)
        p.dma_marks.add(self.ld_n[queue][slot])
        tgt = p.bump(self.ld_n[queue][slot], 16 * n_dma)
        self.fr_expect[slot] = p.bump(self.fr_n[slot], free_amt)
        return slot, buf, (ld_sem, tgt), fr_sem

    def fr_waits(self):
        return [(self.fr_n[i], self.fr_expect[i]) for i in range(self.n)]


def tile_w(W):
    IN, OUT = W.shape
    t = W.reshape(IN // 128, 128, OUT // 128, 128).transpose(2, 1, 0, 3)
    return np.ascontiguousarray(t.reshape(OUT, IN))


def tile_wv(WvT):
    t = WvT.reshape(32, 128, 4, 256).transpose(2, 1, 0, 3)
    return np.ascontiguousarray(t.reshape(512, 8192))


def _pairs_mat():
    cols = []
    for i in range(8):
        for j in range(i + 1, 8):
            c = np.zeros(8, np.float32)
            c[i] = 1.0
            c[j] = 1.0
            cols.append(c)
    return np.stack(cols, axis=1)



def plan_ln_pass(p, env, src, t0, y_dst):
    p4, p5, p6, p7 = env["p4"], env["p5"], env["p6"], env["p7"]
    xs = env["xs_ring"]
    xbf, sqs = env["xbf"], env["sqs"]
    ones_bf, ones_row = env["ones_bf"], env["ones_row"]
    rows = env["rows_sb"]
    T = 512

    stat_base = p.val("PE_STATF")
    for hc in range(HC):
        slot, buf, (lds, ldt), fr = xs.produce(
            "sync",
            lambda eng, dst, hc=hc: eng.dma_start(
                dst[:, :], src[hc * 128:(hc + 1) * 128, t0:t0 + T]),
            free_amt=1)
        xb = xbf[hc % 2]
        sq = sqs[hc % 2]
        pe_need = stat_base + hc - 1 if hc >= 2 else 0

        def fv(eng, buf=buf, xb=xb, sq=sq, lds=lds, ldt=ldt, fr=fr, pe_need=pe_need):
            eng.wait_ge(lds, ldt)
            if pe_need:
                eng.wait_ge(p.sems["PE_STATF"], pe_need)
            eng.tensor_copy(xb[:, :], buf[:, :]).then_inc(fr, 1)
            eng.tensor_mul(sq[:, :], xb[:, :], xb[:, :]).then_inc(
                p.sems["DVE_CAST"], 1)
        dve_tgt = p.bump("DVE_CAST")
        p.add("vector", fv)

        def ft(eng, xb=xb, sq=sq, hc=hc, dve_tgt=dve_tgt):
            eng.wait_ge(p.sems["DVE_CAST"], dve_tgt)
            eng.matmul(p4[0:1, :], ones_bf[:, 0:1], xb[:, :],
                       start=(hc == 0), stop=(hc == HC - 1))
            eng.matmul(p5[0:1, :], ones_bf[:, 0:1], sq[:, :],
                       start=(hc == 0), stop=(hc == HC - 1)).then_inc(
                           p.sems["PE_STATF"], 1)
        p.bump("PE_STATF")
        p.add("tensor", ft)

    pe_stat = p.val("PE_STATF")

    def fv_rows(eng, pe_stat=pe_stat):
        eng.wait_ge(p.sems["PE_STATF"], pe_stat)
        eng.tensor_scalar(rows[0:1, 0:512], p4[0:1, :], 1.0 / HID, None, op0=ALU.mult)
        eng.tensor_scalar(rows[0:1, 512:1024], p5[0:1, :], 1.0 / HID, None, op0=ALU.mult)
        eng.tensor_mul(rows[0:1, 1024:1536], rows[0:1, 0:512], rows[0:1, 0:512])
        eng.tensor_sub(rows[0:1, 1024:1536], rows[0:1, 512:1024],
                       rows[0:1, 1024:1536])
        eng.tensor_scalar(rows[0:1, 1024:1536], rows[0:1, 1024:1536], EPS, None,
                          op0=ALU.add).then_inc(p.sems["DVE_ROWS"], 1)
    p.bump("DVE_ROWS")
    p.add("vector", fv_rows)

    act_t = p.val("DVE_ROWS")

    def fs_rows(eng, act_t=act_t):
        eng.wait_ge(p.sems["DVE_ROWS"], act_t)
        eng.activation(rows[0:1, 1536:2048], rows[0:1, 1024:1536],
                       AF.Sqrt).then_inc(p.sems["ACT_ROWS"], 1)
    p.bump("ACT_ROWS")
    p.add("scalar", fs_rows)

    act_v = p.val("ACT_ROWS")

    def fv_recip(eng, act_v=act_v):
        eng.wait_ge(p.sems["ACT_ROWS"], act_v)
        eng.reciprocal(rows[0:1, 512:1024], rows[0:1, 1536:2048]).then_inc(
            p.sems["DVE_ROWS"], 1)
    p.bump("DVE_ROWS")
    p.add("vector", fv_recip)

    dve_rows = p.val("DVE_ROWS")
    bc_need = p.val("BC_FREE")

    def ft_bc(eng, dve_rows=dve_rows, bc_need=bc_need):
        eng.wait_ge(p.sems["DVE_ROWS"], dve_rows)
        if bc_need:
            eng.wait_ge(p.sems["BC_FREE"], bc_need)
        eng.matmul(p6[:, :], ones_row[0:1, :], rows[0:1, 0:512], start=True, stop=True)
        eng.matmul(p7[:, :], ones_row[0:1, :], rows[0:1, 512:1024],
                   start=True, stop=True).then_inc(p.sems["PE_BC"], 1)
    pe_bc = p.bump("PE_BC")
    p.add("tensor", ft_bc)

    for hc in range(HC):
        slot, buf, (lds, ldt), fr = xs.produce(
            "sync",
            lambda eng, dst, hc=hc: eng.dma_start(
                dst[:, :], src[hc * 128:(hc + 1) * 128, t0:t0 + T]),
            free_amt=1)
        ydst = y_dst(hc)
        last = (hc == HC - 1)

        def fv(eng, buf=buf, ydst=ydst, lds=lds, ldt=ldt, fr=fr, pe_bc=pe_bc,
               last=last):
            eng.wait_ge(lds, ldt)
            eng.wait_ge(p.sems["PE_BC"], pe_bc)
            eng.tensor_sub(buf[:, :], buf[:, :], p6[:, :])
            i = eng.tensor_mul(ydst, buf[:, :], p7[:, :])
            i.then_inc(fr, 1)
            if last:
                eng.engine_nop().then_inc(p.sems["DVE_Y"], 1)
                eng.engine_nop().then_inc(p.sems["BC_FREE"], 1)
        p.add("vector", fv)
    p.bump("DVE_Y")
    p.bump("BC_FREE")


def plan_wcol_load(p, env, key, rows0, ncols, frees_pe, ring=None,
                   extra_waits=()):
    mode, src = key[0], key[1]
    wr = ring if ring is not None else env["w_ring"]
    if mode == "bf":
        slot, buf, ld, fr = wr.produce(
            "sync",
            lambda eng, dst: eng.dma_start(
                dst[:, 0:ncols], src[rows0:rows0 + 128, 0:ncols]),
            free_amt=frees_pe, extra_waits=extra_waits)
    else:
        spill_amt = 16 if mode == "cast_spill" else 0
        slot, buf, ld, fr = wr.produce(
            "gpsimd",
            lambda eng, dst: eng.dma_start(
                dst[:, 0:ncols], src[rows0:rows0 + 128, 0:ncols]),
            free_amt=frees_pe + spill_amt, extra_waits=extra_waits)
        if mode == "cast_spill":
            spill = key[2]
            lds, ldt = ld

            def fsp(eng, buf=buf, lds=lds, ldt=ldt, fr=fr, rows0=rows0,
                    ncols=ncols, spill=spill):
                eng.wait_ge(lds, ldt)
                eng.dma_start(spill[rows0:rows0 + 128, 0:ncols],
                              buf[:, 0:ncols]).then_inc(fr, 16)
            p.add("sync", fsp)
            p.dma_marks.add(wr.fr_n[slot])
    return buf, ld, fr


def _sync_end_wait(p, stg):
    waits = [(stg.fr_n[i], p.val(stg.fr_n[i])) for i in range(stg.n)]

    def f_end(eng, waits=tuple(waits)):
        for name, v in waits:
            eng.wait_ge(p.sems[name], v)
    p.add("sync", f_end)



def plan_ffn_phase(p, env, src, dst, wkeys_fn):
    mp = env["main_ps"]
    p4, p5, p6, p7 = env["p4"], env["p5"], env["p6"], env["p7"]
    ones_bf, ones_row = env["ones_bf"], env["ones_row"]
    rows = env["rows_sb"]
    y1s, hx, moe = env["y1s"], env["hx"], env["moe"]
    xs, stg = env["xs_ring"], env["stg_ring"]
    ztmp, sqs = env["ztmp"], env["sqs"]
    logits_bf, pairs_bf = env["logits_bf"], env["pairs_bf"]
    rwcols, rwrow_sb, rwb_sb = env["rwcols"], env["rwrow_sb"], env["rwb_sb"]
    gate_sb, p28 = env["gate_sb"], env["p28_sb"]
    id_bf, id_f32, gb2 = env["id_bf"], env["id_f32"], env["gb2"]
    main_rec = env.setdefault("main_rec", {})
    gidx = env.setdefault("gidx", [0])

    def pe_group(emit_mms, evac_engine, emit_evac, frees=()):
        g = gidx[0]
        gidx[0] += 1
        slot = g % 4
        ps = mp[:, slot * 512:(slot + 1) * 512]
        prev = main_rec.get(slot)
        pe_t = p.bump("PE_G")

        def ft(eng, prev=prev, ps=ps):
            if prev is not None:
                eng.wait_ge(p.sems[prev[0]], prev[1])
            emit_mms(eng, ps).then_inc(p.sems["PE_G"], 1)
        p.add("tensor", ft)
        sem_name = {"vector": "DVE_EVAC", "scalar": "ACT_EVAC"}[evac_engine]
        ev_t = p.bump(sem_name)

        def fe(eng, pe_t=pe_t, ps=ps, frees=tuple(frees)):
            eng.wait_ge(p.sems["PE_G"], pe_t)
            emit_evac(eng, ps).then_inc(p.sems[sem_name], 1)
            for sem, amt in frees:
                eng.nop().then_inc(sem, amt)
        p.add(evac_engine, fe)
        main_rec[slot] = (sem_name, ev_t)
        return ps

    for t in range(NTA):
        t0 = t * TA
        wkeys = wkeys_fn(t)
        plan_ln_pass(p, env, src, t0, lambda hc: y1s[:, hc * 512:(hc + 1) * 512])
        dve_y = p.val("DVE_Y")
        if env.get("stop_stage") == "ln":
            continue

        def mk_gate(eng, ps, dve_y=dve_y):
            eng.wait_ge(p.sems["DVE_Y"], dve_y)
            for hc in range(HC):
                i = eng.matmul(ps[0:8, :], gate_sb[:, hc * 8:(hc + 1) * 8],
                               y1s[:, hc * 512:(hc + 1) * 512],
                               start=(hc == 0), stop=(hc == HC - 1))
            return i

        def ev_gate(eng, ps):
            return eng.tensor_copy(logits_bf[0:8, :], ps[0:8, :])
        pe_group(mk_gate, "vector", ev_gate)
        dve_lg = p.val("DVE_EVAC")
        if env.get("stop_stage") == "g1":
            continue

        def mk_pairs(eng, ps, dve_lg=dve_lg):
            eng.wait_ge(p.sems["DVE_EVAC"], dve_lg)
            return eng.matmul(ps[0:28, :], p28[0:8, :], logits_bf[0:8, :],
                              start=True, stop=True)

        def ev_pairs(eng, ps):
            return eng.tensor_copy(pairs_bf[0:28, :], ps[0:28, :])
        pe_group(mk_pairs, "vector", ev_pairs)
        dve_pr = p.val("DVE_EVAC")
        if env.get("stop_stage") == "g2":
            continue

        def mk_rwtp(eng, ps, dve_pr=dve_pr):
            eng.wait_ge(p.sems["DVE_EVAC"], dve_pr)
            for c in range(4):
                i = eng.matmul(ps[:, c * 32:c * 32 + 28],
                               pairs_bf[0:28, c * 128:(c + 1) * 128],
                               id_bf[0:28, 0:28], start=True, stop=True)
            return i

        def ev_rwtp(eng, ps):
            for c in range(4):
                i = eng.reduce_max(rwcols[:, c:c + 1], ps[:, c * 32:c * 32 + 28],
                                   axis=AX.X)
            return i
        pe_group(mk_rwtp, "vector", ev_rwtp)
        dve_rw = p.val("DVE_EVAC")
        if env.get("stop_stage") == "g3":
            continue

        def mk_rwrow(eng, ps, dve_rw=dve_rw):
            eng.wait_ge(p.sems["DVE_EVAC"], dve_rw)
            for c in range(4):
                i = eng.matmul(ps[0:1, c * 128:(c + 1) * 128], rwcols[:, c:c + 1],
                               id_f32[:, :], start=True, stop=True)
            return i

        def ev_rwrow(eng, ps):
            return eng.tensor_copy(rwrow_sb[0:1, :], ps[0:1, :])
        pe_group(mk_rwrow, "vector", ev_rwrow)
        dve_rr = p.val("DVE_EVAC")
        if env.get("stop_stage") == "g4":
            continue
        bc_need = p.val("BC_FREE")

        def ft_rwb(eng, dve_rr=dve_rr, bc_need=bc_need):
            eng.wait_ge(p.sems["DVE_EVAC"], dve_rr)
            if bc_need:
                eng.wait_ge(p.sems["BC_FREE"], bc_need)
            eng.matmul(p6[:, :], ones_row[0:1, :], rwrow_sb[0:1, :],
                       start=True, stop=True).then_inc(p.sems["PE_BC"], 1)
        pe_bc_rw = p.bump("PE_BC")
        p.add("tensor", ft_rwb)

        def fv_rwb(eng, pe_bc_rw=pe_bc_rw):
            eng.wait_ge(p.sems["PE_BC"], pe_bc_rw)
            eng.tensor_copy(rwb_sb[:, :], p6[:, :]).then_inc(p.sems["BC_FREE"], 1)
        p.bump("BC_FREE")
        p.add("vector", fv_rwb)

        if env.get("stop_stage") == "gate":
            continue
        for lc in range(LC):
            buf, (lds, ldt), fr = plan_wcol_load(
                p, env, wkeys["fc1"], lc * 128, HID, 1, ring=env["ring_sm"],
                extra_waits=(env["ring_dn"].fr_waits() if lc == 0 else ()))

            def mk(eng, ps, buf=buf, lds=lds, ldt=ldt, dve_y=dve_y):
                eng.wait_ge(lds, ldt)
                eng.wait_ge(p.sems["DVE_Y"], dve_y)
                for hc in range(HC):
                    i = eng.matmul(ps[:, :], buf[:, hc * 128:(hc + 1) * 128],
                                   y1s[:, hc * 512:(hc + 1) * 512],
                                   start=(hc == 0), stop=(hc == HC - 1))
                return i

            def ev(eng, ps, lc=lc):
                return eng.tensor_copy(hx[:, lc * 512:(lc + 1) * 512], ps[:, :])
            pe_group(mk, "vector", ev, frees=[(fr, 1)])
        dve_xlat = p.val("DVE_EVAC")

        for mc in range(LC):
            buf, (lds, ldt), fr = plan_wcol_load(p, env, wkeys["expert"],
                                                 mc * 128, LAT, 1,
                                                 ring=env["ring_sm"])

            def mk(eng, ps, buf=buf, lds=lds, ldt=ldt, dve_xlat=dve_xlat):
                eng.wait_ge(lds, ldt)
                eng.wait_ge(p.sems["DVE_EVAC"], dve_xlat)
                for lc in range(LC):
                    i = eng.matmul(ps[:, :], buf[:, lc * 128:(lc + 1) * 128],
                                   hx[:, lc * 512:(lc + 1) * 512],
                                   start=(lc == 0), stop=(lc == LC - 1))
                return i

            def ev(eng, ps, mc=mc):
                return eng.tensor_mul(moe[:, mc * 512:(mc + 1) * 512],
                                      rwb_sb[:, :], ps[:, :])
            pe_group(mk, "vector", ev, frees=[(fr, 1)])

        if env.get("stop_stage") == "lat":
            continue
        for fc in range(IC):
            buf, (lds, ldt), fr = plan_wcol_load(p, env, wkeys["up"], fc * 128,
                                                 HID, 1, ring=env["ring_sm"])

            def mk(eng, ps, buf=buf, lds=lds, ldt=ldt, dve_y=dve_y):
                eng.wait_ge(lds, ldt)
                eng.wait_ge(p.sems["DVE_Y"], dve_y)
                for hc in range(HC):
                    i = eng.matmul(ps[:, :], buf[:, hc * 128:(hc + 1) * 128],
                                   y1s[:, hc * 512:(hc + 1) * 512],
                                   start=(hc == 0), stop=(hc == HC - 1))
                return i

            def ev(eng, ps, fc=fc):
                return eng.activation(hx[:, fc * 512:(fc + 1) * 512], ps[:, :],
                                      AF.Relu)
            pe_group(mk, "scalar", ev, frees=[(fr, 1)])
            act_h = p.val("ACT_EVAC")

            def fv_sq(eng, fc=fc, act_h=act_h):
                eng.wait_ge(p.sems["ACT_EVAC"], act_h)
                sl = hx[:, fc * 512:(fc + 1) * 512]
                eng.tensor_mul(sl, sl, sl).then_inc(p.sems["DVE_H"], 1)
            p.bump("DVE_H")
            p.add("vector", fv_sq)
        dve_h = p.val("DVE_H")

        if env.get("stop_stage") == "up":
            continue
        for gc in range(HC):
            kd, kf = wkeys["down"], wkeys["fc2"]
            mode = kd[0]
            wr = env["ring_dn"]
            extra = env["ring_sm"].fr_waits() if gc == 0 else ()
            spill_amt = 32 if mode == "cast_spill" else 0

            def dma_fn(eng, dst, kd=kd, kf=kf, gc=gc):
                i1 = eng.dma_start(dst[:, 0:INT],
                                   kd[1][gc * 128:(gc + 1) * 128, 0:INT])
                i2 = eng.dma_start(dst[:, INT:INT + LAT],
                                   kf[1][gc * 128:(gc + 1) * 128, 0:LAT])
                return [i1, i2]
            slot, buf, (lds, ldt), fr = wr.produce(
                "sync" if mode == "bf" else "gpsimd", dma_fn,
                free_amt=1 + spill_amt, n_dma=2, extra_waits=extra)
            if mode == "cast_spill":
                def fsp(eng, buf=buf, lds=lds, ldt=ldt, fr=fr, gc=gc,
                        kd=kd, kf=kf):
                    eng.wait_ge(lds, ldt)
                    eng.dma_start(kd[2][gc * 128:(gc + 1) * 128, 0:INT],
                                  buf[:, 0:INT]).then_inc(fr, 16)
                    eng.dma_start(kf[2][gc * 128:(gc + 1) * 128, 0:LAT],
                                  buf[:, INT:INT + LAT]).then_inc(fr, 16)
                p.add("sync", fsp)
                p.dma_marks.add(wr.fr_n[slot])

            def mk(eng, ps, buf=buf, lds=lds, ldt=ldt, dve_h=dve_h):
                eng.wait_ge(lds, ldt)
                eng.wait_ge(p.sems["DVE_H"], dve_h)
                for fc in range(IC):
                    eng.matmul(ps[:, :], buf[:, fc * 128:(fc + 1) * 128],
                               hx[:, fc * 512:(fc + 1) * 512],
                               start=(fc == 0), stop=False)
                for mc in range(LC):
                    i = eng.matmul(ps[:, :],
                                   buf[:, INT + mc * 128:INT + (mc + 1) * 128],
                                   moe[:, mc * 512:(mc + 1) * 512],
                                   start=False, stop=(mc == LC - 1))
                return i

            def ev(eng, ps, gc=gc):
                i = eng.tensor_copy(y1s[:, gc * 512:(gc + 1) * 512], ps[:, :])
                sl = y1s[:, gc * 512:(gc + 1) * 512]
                eng.tensor_mul(sqs[0][:, :], sl, sl).then_inc(p.sems["DVE_SBF"], 1)
                return i
            p.bump("DVE_SBF")
            pe_group(mk, "vector", ev, frees=[(fr, 1)])
            dve_sbf = p.val("DVE_SBF")

            def ft_stat(eng, gc=gc, dve_sbf=dve_sbf):
                eng.wait_ge(p.sems["DVE_SBF"], dve_sbf)
                eng.matmul(p4[0:1, :], ones_bf[:, 0:1],
                           y1s[:, gc * 512:(gc + 1) * 512],
                           start=(gc == 0), stop=(gc == HC - 1))
                eng.matmul(p5[0:1, :], ones_bf[:, 0:1], sqs[0][:, :],
                           start=(gc == 0), stop=(gc == HC - 1)).then_inc(
                               p.sems["PE_STATF"], 1)
            p.bump("PE_STATF")
            p.add("tensor", ft_stat)

        if env.get("stop_stage") == "down":
            continue
        pe_stat = p.val("PE_STATF")

        def fv_rows2(eng, pe_stat=pe_stat):
            eng.wait_ge(p.sems["PE_STATF"], pe_stat)
            eng.tensor_scalar(rows[0:1, 0:512], p4[0:1, :], 1.0 / HID, None,
                              op0=ALU.mult)
            eng.tensor_scalar(rows[0:1, 512:1024], p5[0:1, :], 1.0 / HID, None,
                              op0=ALU.mult)
            eng.tensor_mul(rows[0:1, 1024:1536], rows[0:1, 0:512], rows[0:1, 0:512])
            eng.tensor_sub(rows[0:1, 1024:1536], rows[0:1, 512:1024],
                           rows[0:1, 1024:1536]).then_inc(p.sems["DVE_ROWS"], 1)
        p.bump("DVE_ROWS")
        p.add("vector", fv_rows2)
        act_t2 = p.val("DVE_ROWS")

        def fs_rows2(eng, act_t2=act_t2):
            eng.wait_ge(p.sems["DVE_ROWS"], act_t2)
            eng.activation(rows[0:1, 1536:2048], rows[0:1, 1024:1536],
                           AF.Sqrt).then_inc(p.sems["ACT_ROWS"], 1)
        p.bump("ACT_ROWS")
        p.add("scalar", fs_rows2)
        act_v2 = p.val("ACT_ROWS")

        def fv_recip2(eng, act_v2=act_v2):
            eng.wait_ge(p.sems["ACT_ROWS"], act_v2)
            eng.reciprocal(rows[0:1, 512:1024], rows[0:1, 1536:2048]).then_inc(
                p.sems["DVE_ROWS"], 1)
        p.bump("DVE_ROWS")
        p.add("vector", fv_recip2)
        dve_rows2 = p.val("DVE_ROWS")
        bc_need2 = p.val("BC_FREE")

        def ft_bc2(eng, dve_rows2=dve_rows2, bc_need2=bc_need2):
            eng.wait_ge(p.sems["DVE_ROWS"], dve_rows2)
            if bc_need2:
                eng.wait_ge(p.sems["BC_FREE"], bc_need2)
            eng.matmul(p6[:, :], ones_row[0:1, :], rows[0:1, 0:512],
                       start=True, stop=True)
            eng.matmul(p7[:, :], ones_row[0:1, :], rows[0:1, 512:1024],
                       start=True, stop=True).then_inc(p.sems["PE_BC"], 1)
        pe_bc2 = p.bump("PE_BC")
        p.add("tensor", ft_bc2)

        for gc in range(HC):
            slot, buf, (lds, ldt), fr = xs.produce(
                "sync",
                lambda eng, dstb, gc=gc, t0=t0: eng.dma_start(
                    dstb[:, :], src[gc * 128:(gc + 1) * 128, t0:t0 + TA]),
                free_amt=1)
            st_slot = stg.i
            stg.i = (stg.i + 1) % stg.n
            sbuf = stg.slots[st_slot]
            need_free = stg.fr_expect[st_slot]
            last = (gc == HC - 1)

            def fv(eng, buf=buf, sbuf=sbuf, gc=gc, lds=lds, ldt=ldt, fr=fr,
                   pe_bc2=pe_bc2, need_free=need_free, st_slot=st_slot,
                   last=last):
                eng.wait_ge(lds, ldt)
                eng.wait_ge(p.sems["PE_BC"], pe_bc2)
                if need_free:
                    eng.wait_ge(stg.fr[st_slot], need_free)
                sl = y1s[:, gc * 512:(gc + 1) * 512]
                eng.tensor_sub(ztmp[:, :], sl, p6[:, :])
                eng.tensor_mul(ztmp[:, :], ztmp[:, :], p7[:, :])
                eng.tensor_scalar(ztmp[:, :], ztmp[:, :], gb2[:, gc:gc + 1],
                                  gb2[:, HC + gc:HC + gc + 1],
                                  op0=ALU.mult, op1=ALU.add)
                eng.tensor_add(sbuf[:, :], buf[:, :], ztmp[:, :]).then_inc(fr, 1)
                eng.engine_nop().then_inc(stg.ld["eng"][st_slot], 1)
                if last:
                    eng.engine_nop().then_inc(p.sems["BC_FREE"], 1)
            rdy = p.bump(stg.ld_n["eng"][st_slot], 1)
            p.add("vector", fv)
            if last:
                p.bump("BC_FREE")
            p.dma_marks.add(stg.fr_n[st_slot])
            stg.fr_expect[st_slot] = p.bump(stg.fr_n[st_slot], 16)

            def fst(eng, sbuf=sbuf, gc=gc, t0=t0, st_slot=st_slot, rdy=rdy):
                eng.wait_ge(stg.ld["eng"][st_slot], rdy)
                eng.dma_start(dst[gc * 128:(gc + 1) * 128, t0:t0 + TA],
                              sbuf[:, :]).then_inc(stg.fr[st_slot], 16)
            p.add("gpsimd", fst)

        if env.get("tile_barrier"):
            p.barrier()
        snap = env.get("snap")
        if snap is not None and t == 0:
            waits = [(stg.fr_n[i], p.val(stg.fr_n[i])) for i in range(stg.n)]

            def f_snap(eng, waits=tuple(waits)):
                for name, v in waits:
                    eng.wait_ge(p.sems[name], v)
                eng.dma_start(snap[:, :], dst[:, 0:TA]).then_inc(
                    p.sems["SNAP"], 16)
            p.dma_marks.add("SNAP")
            p.bump("SNAP", 16)
            p.add("sync", f_snap)

    _sync_end_wait(p, stg)



def plan_qkv_phase(p, env, src, wq, wk, wv, q_scr, K_sb, V_sb, bvb_sb):
    mp = env["main_ps"]
    y2 = env["y2"]
    stg = env["stg_ring"]
    main_rec = env.setdefault("main_rec", {})
    gidx = env.setdefault("gidx", [0])

    def pe_group(emit_mms, emit_evac, frees=()):
        g = gidx[0]
        gidx[0] += 1
        slot = g % 4
        ps = mp[:, slot * 512:(slot + 1) * 512]
        prev = main_rec.get(slot)
        pe_t = p.bump("PE_G")

        def ft(eng, prev=prev, ps=ps):
            if prev is not None:
                eng.wait_ge(p.sems[prev[0]], prev[1])
            emit_mms(eng, ps).then_inc(p.sems["PE_G"], 1)
        p.add("tensor", ft)
        ev_t = p.bump("DVE_EVAC")

        def fe(eng, pe_t=pe_t, ps=ps, frees=tuple(frees)):
            eng.wait_ge(p.sems["PE_G"], pe_t)
            emit_evac(eng, ps).then_inc(p.sems["DVE_EVAC"], 1)
            for sem, amt in frees:
                eng.nop().then_inc(sem, amt)
        p.add("vector", fe)
        main_rec[slot] = ("DVE_EVAC", ev_t)

    for tile in range(NTB):
        for tb in range(2):
            t0 = tile * TB + tb * 512
            plan_ln_pass(p, env, src, t0,
                         lambda hc, tb=tb: y2[:, hc * TB + tb * 512:
                                              hc * TB + (tb + 1) * 512])
        dve_y = p.val("DVE_Y")

        for fc in range(HC):
            buf, (lds, ldt), fr = plan_wcol_load(p, env, wq, fc * 128, HID, 2)
            for tb in range(2):
                def mk(eng, ps, buf=buf, lds=lds, ldt=ldt, tb=tb,
                       dve_y=dve_y):
                    eng.wait_ge(lds, ldt)
                    eng.wait_ge(p.sems["DVE_Y"], dve_y)
                    for hc in range(HC):
                        i = eng.matmul(
                            ps[:, :], buf[:, hc * 128:(hc + 1) * 128],
                            y2[:, hc * TB + tb * 512: hc * TB + (tb + 1) * 512],
                            start=(hc == 0), stop=(hc == HC - 1))
                    return i

                st_slot = stg.i
                stg.i = (stg.i + 1) % stg.n
                sbuf = stg.slots[st_slot]
                need_free = stg.fr_expect[st_slot]
                rdy = p.bump(stg.ld_n["eng"][st_slot], 1)
                p.dma_marks.add(stg.fr_n[st_slot])
                stg.fr_expect[st_slot] = p.bump(stg.fr_n[st_slot], 16)

                def ev(eng, ps, sbuf=sbuf, st_slot=st_slot, need_free=need_free):
                    if need_free:
                        eng.wait_ge(stg.fr[st_slot], need_free)
                    i = eng.tensor_scalar(sbuf[:, :], ps[:, :], QSCALE, None,
                                          op0=ALU.mult)
                    eng.engine_nop().then_inc(stg.ld["eng"][st_slot], 1)
                    return i
                pe_group(mk, ev, frees=[(fr, 1)])
                t0q = tile * TB + tb * 512

                def fst(eng, sbuf=sbuf, fc=fc, t0q=t0q, st_slot=st_slot, rdy=rdy):
                    eng.wait_ge(stg.ld["eng"][st_slot], rdy)
                    eng.dma_start(q_scr[fc * 128:(fc + 1) * 128, t0q:t0q + 512],
                                  sbuf[:, :]).then_inc(stg.fr[st_slot], 16)
                p.add("gpsimd", fst)

        for fc in range(NKV):
            buf, (lds, ldt), fr = plan_wcol_load(p, env, wk, fc * 128, HID, 2)
            for tb in range(2):
                def mk(eng, ps, buf=buf, lds=lds, ldt=ldt, tb=tb,
                       dve_y=dve_y):
                    eng.wait_ge(lds, ldt)
                    eng.wait_ge(p.sems["DVE_Y"], dve_y)
                    for hc in range(HC):
                        i = eng.matmul(
                            ps[:, :], buf[:, hc * 128:(hc + 1) * 128],
                            y2[:, hc * TB + tb * 512: hc * TB + (tb + 1) * 512],
                            start=(hc == 0), stop=(hc == HC - 1))
                    return i

                t0k = tile * TB + tb * 512

                def ev(eng, ps, fc=fc, t0k=t0k):
                    return eng.tensor_copy(
                        K_sb[:, fc * S + t0k: fc * S + t0k + 512], ps[:, :])
                pe_group(mk, ev, frees=[(fr, 1)])

        for dvq in range(4):
            buf, (lds, ldt), fr = plan_wcol_load(p, env, wv, dvq * 128, INT, 8)
            for tc in range(8):
                tglob = tile * 8 + tc
                last = (tc == 7)

                def mk(eng, ps, buf=buf, lds=lds, ldt=ldt, tc=tc,
                       dve_y=dve_y):
                    eng.wait_ge(lds, ldt)
                    eng.wait_ge(p.sems["DVE_Y"], dve_y)
                    for hc in range(HC):
                        i = eng.matmul(
                            ps[:, 0:256],
                            y2[:, hc * TB + tc * 128: hc * TB + (tc + 1) * 128],
                            buf[:, hc * 256:(hc + 1) * 256],
                            start=(hc == 0), stop=(hc == HC - 1))
                    return i

                def ev(eng, ps, tglob=tglob, dvq=dvq):
                    return eng.tensor_add(
                        V_sb[:, tglob * 1024 + dvq * 256:
                             tglob * 1024 + (dvq + 1) * 256],
                        bvb_sb[:, dvq * 256:(dvq + 1) * 256], ps[:, 0:256])
                pe_group(mk, ev, frees=[(fr, 1)])

    _sync_end_wait(p, stg)



def plan_attn_phase(p, env, q_scr, K_sb, V_sb, ctx_scr):
    mp = env["main_ps"]
    p4, p5, p6 = env["p4"], env["p5"], env["p6"]
    qh = env["qh_ring"]
    P_sb, PT_sb, ctx_sb = env["P_sb"], env["PT_sb"], env["ctx_sb"]
    id_bf = env["id_bf"]
    nm = env["nmax"]
    Dr = env["Dr"]
    recip = env["recip"]
    main_rec = env.setdefault("main_rec", {})
    pt_rec = {0: None, 1: None}
    ctx_rec = [None]
    psb_free = [None]

    PT3 = PT_sb.rearrange("p (j z) -> p j z", z=512)

    for g in range(4):
        if g > 0:
            st_done0 = p.val("CTX_ST")

            def fv_wait(eng, st_done0=st_done0):
                eng.wait_ge(p.sems["CTX_ST"], st_done0)
            p.add("vector", fv_wait)
        def qh_produce(hh, g=g):
            return qh.produce(
                "sync",
                lambda eng, dstb, hh=hh, g=g: eng.dma_start(
                    dstb[:, :], q_scr[hh * 128:(hh + 1) * 128,
                                      g * 512:(g + 1) * 512]),
                free_amt=1)

        next_tok = qh_produce(0)
        for h in range(NQ):
            kv = h // 4
            slot, qbuf, (lds, ldt), qfr = next_tok

            tp_meta = [None] * 4
            for c in range(5):
                if c < 4:
                    par = c % 2
                    nmo = par * 8
                    psb = P_sb[:, par * 2048:(par + 1) * 2048]
                    prevs = [main_rec.get(k) for k in range(4)]
                    p.bump("PE_SC", 4)
                    pe_sc = p.val("PE_SC")

                    def ft(eng, qbuf=qbuf, c=c, kv=kv, lds=lds, ldt=ldt,
                           prevs=tuple(prevs)):
                        eng.wait_ge(lds, ldt)
                        for k in range(4):
                            pr = prevs[k]
                            if pr is not None:
                                eng.wait_ge(p.sems[pr[0]], pr[1])
                            eng.matmul(mp[:, k * 512:(k + 1) * 512],
                                       qbuf[:, c * 128:(c + 1) * 128],
                                       K_sb[:, kv * S + k * 512:
                                            kv * S + (k + 1) * 512],
                                       start=True, stop=True).then_inc(
                                           p.sems["PE_SC"], 1)
                    p.add("tensor", ft)

                    def fv_max(eng, pe_sc=pe_sc, nmo=nmo, qfr=qfr,
                               last_c=(c == 3)):
                        for k in range(4):
                            eng.wait_ge(p.sems["PE_SC"], pe_sc - 3 + k)
                            eng.reduce_max(nm[:, nmo + k:nmo + k + 1],
                                           mp[:, k * 512:(k + 1) * 512],
                                           axis=AX.X, negate=True)
                        eng.tensor_tensor(nm[:, nmo + 4:nmo + 5],
                                          nm[:, nmo:nmo + 1],
                                          nm[:, nmo + 1:nmo + 2], op=ALU.min)
                        eng.tensor_tensor(nm[:, nmo + 5:nmo + 6],
                                          nm[:, nmo + 2:nmo + 3],
                                          nm[:, nmo + 3:nmo + 4], op=ALU.min)
                        eng.tensor_tensor(nm[:, nmo + 4:nmo + 5],
                                          nm[:, nmo + 4:nmo + 5],
                                          nm[:, nmo + 5:nmo + 6],
                                          op=ALU.min).then_inc(
                                              p.sems["DVE_MAX"], 1)
                        if last_c:
                            eng.engine_nop().then_inc(qfr, 1)
                    dve_max = p.bump("DVE_MAX")
                    p.add("vector", fv_max)

                    psbf = env.get("psb_free_%d" % par)

                    def fs_exp(eng, dve_max=dve_max, nmo=nmo, psbf=psbf,
                               psb=psb):
                        eng.wait_ge(p.sems["DVE_MAX"], dve_max)
                        if psbf is not None:
                            eng.wait_ge(p.sems[psbf[0]], psbf[1])
                        eng.activation(psb[:, 0:1024], mp[:, 0:1024], AF.Exp,
                                       bias=nm[:, nmo + 4:nmo + 5], scale=1.0,
                                       accum_out=nm[:, nmo + 6:nmo + 7]
                                       ).then_inc(p.sems["ACT_P"], 1)
                        eng.activation(psb[:, 1024:2048], mp[:, 1024:2048],
                                       AF.Exp,
                                       bias=nm[:, nmo + 4:nmo + 5], scale=1.0,
                                       accum_out=nm[:, nmo + 7:nmo + 8]
                                       ).then_inc(p.sems["ACT_P"], 1)
                    p.bump("ACT_P", 2)
                    act_p = p.val("ACT_P")
                    p.add("scalar", fs_exp)
                    main_rec[0] = ("ACT_P", act_p - 1)
                    main_rec[1] = ("ACT_P", act_p - 1)
                    main_rec[2] = ("ACT_P", act_p)
                    main_rec[3] = ("ACT_P", act_p)

                    r1 = p.bump("DVE_RCP")
                    r2 = p.bump("DVE_RCP")

                    def fv_rcp(eng, act_p=act_p, nmo=nmo, par=par, r1=r1,
                               r2=r2, psb=psb):
                        eng.wait_ge(p.sems["ACT_P"], act_p)
                        eng.tensor_add(recip[:, par:par + 1],
                                       nm[:, nmo + 6:nmo + 7],
                                       nm[:, nmo + 7:nmo + 8]).then_inc(
                                           p.sems["DVE_RCP"], 1)
                        eng.wait_ge(p.sems["DVE_RCP"], r1)
                        eng.reciprocal(recip[:, par:par + 1],
                                       recip[:, par:par + 1]).then_inc(
                                           p.sems["DVE_RCP"], 1)
                        eng.wait_ge(p.sems["DVE_RCP"], r2)
                        eng.tensor_scalar(Dr[:, par * 128:(par + 1) * 128],
                                          id_bf[:, :],
                                          recip[:, par:par + 1], None,
                                          op0=ALU.mult).then_inc(
                                              p.sems["GP_PN"], 1)
                    gp_pn = p.bump("GP_PN")
                    p.add("vector", fv_rcp)
                    tp_meta[c] = (par, gp_pn, psb)

                if c >= 1:
                    cc = c - 1
                    par, gp_pn, psb = tp_meta[cc]
                    for jq in range(4):
                        ptb = p4 if jq % 2 == 0 else p5
                        prev_pt = pt_rec[jq % 2]
                        p.bump("PE_PT2", 4)
                        pe_pt2 = p.val("PE_PT2")

                        def ft_tp(eng, jq=jq, gp_pn=gp_pn, prev_pt=prev_pt,
                                  ptb=ptb, psb=psb, par=par):
                            eng.wait_ge(p.sems["GP_PN"], gp_pn)
                            if prev_pt is not None:
                                eng.wait_ge(p.sems[prev_pt[0]], prev_pt[1])
                            for jj in range(4):
                                j = jq * 4 + jj
                                eng.matmul(ptb[:, jj * 128:(jj + 1) * 128],
                                           psb[:, j * 128:(j + 1) * 128],
                                           Dr[:, par * 128:(par + 1) * 128],
                                           start=True,
                                           stop=True).then_inc(
                                               p.sems["PE_PT2"], 1)
                        p.add("tensor", ft_tp)
                        ev_eng = "vector" if jq % 2 == 0 else "scalar"
                        sem_name = {"vector": "DVE_EVAC",
                                    "scalar": "ACT_EVAC"}[ev_eng]
                        ev_t = p.bump(sem_name)

                        def fe_tp(eng, jq=jq, cc=cc, pe_pt2=pe_pt2, ptb=ptb,
                                  sem_name_=sem_name):
                            eng.wait_ge(p.sems["PE_PT2"], pe_pt2)
                            dst = PT3[:, jq * 4:(jq + 1) * 4,
                                      cc * 128:(cc + 1) * 128]
                            srcv = ptb.rearrange("p (j z) -> p j z",
                                                 z=128)[:, :, :]
                            if hasattr(eng, "tensor_copy"):
                                eng.tensor_copy(dst, srcv).then_inc(
                                    p.sems[sem_name_], 1)
                            else:
                                eng.copy(dst, srcv).then_inc(
                                    p.sems[sem_name_], 1)
                        p.add(ev_eng, fe_tp)
                        pt_rec[jq % 2] = (sem_name, ev_t)
                    env["psb_free_%d" % par] = ("PE_PT2", p.val("PE_PT2"))

            pt_done = [("DVE_EVAC", p.val("DVE_EVAC")),
                       ("ACT_EVAC", p.val("ACT_EVAC"))]
            prev_ctx = ctx_rec[0]
            pe_ctx2 = p.bump("PE_CTX2")

            def ft_ctx(eng, kv=kv, pt_done=tuple(pt_done), prev_ctx=prev_ctx):
                for name, v in pt_done:
                    eng.wait_ge(p.sems[name], v)
                if prev_ctx is not None:
                    eng.wait_ge(p.sems[prev_ctx[0]], prev_ctx[1])
                for j in range(16):
                    i = eng.matmul(
                        p6[:, :],
                        V_sb[:, j * 1024 + kv * 128: j * 1024 + (kv + 1) * 128],
                        PT_sb[:, j * 512:(j + 1) * 512],
                        start=(j == 0), stop=(j == 15))
                i.then_inc(p.sems["PE_CTX2"], 1)
            p.add("tensor", ft_ctx)

            def fv_ctx(eng, h=h, pe_ctx2=pe_ctx2):
                eng.wait_ge(p.sems["PE_CTX2"], pe_ctx2)
                eng.tensor_copy(ctx_sb[:, h * 512:(h + 1) * 512],
                                p6[:, :]).then_inc(p.sems["DVE_CTX"], 1)
            dve_ctx = p.bump("DVE_CTX")
            p.add("vector", fv_ctx)
            ctx_rec[0] = ("DVE_CTX", dve_ctx)

            if h + 1 < NQ:
                next_tok = qh_produce(h + 1)

            def fst(eng, h=h, g=g, dve_ctx=dve_ctx):
                eng.wait_ge(p.sems["DVE_CTX"], dve_ctx)
                eng.dma_start(ctx_scr[h * 128:(h + 1) * 128,
                                      g * 512:(g + 1) * 512],
                              ctx_sb[:, h * 512:(h + 1) * 512]).then_inc(
                                  p.sems["CTX_ST"], 16)
            p.dma_marks.add("CTX_ST"); p.bump("CTX_ST", 16)
            p.add("sync", fst)

            if g == 0 and h == 0 and env.get("attn_dbg") is not None:
                pdbg, ptdbg = env["attn_dbg"]
                ctx_done = p.val("DVE_CTX")

                def f_pd(eng, ctx_done=ctx_done):
                    eng.wait_ge(p.sems["DVE_CTX"], ctx_done)
                    eng.dma_start(pdbg[:, :], P_sb[:, 2048:4096]).then_inc(
                        p.sems["P0G"], 16)
                    eng.dma_start(ptdbg[:, :], PT_sb[:, :]).then_inc(
                        p.sems["P0G"], 16)
                    eng.dma_start(env["nm_dbg"][:, 0:16], nm[:, :]).then_inc(
                        p.sems["P0G"], 16)
                    eng.dma_start(env["nm_dbg"][:, 16:18],
                                  recip[:, :]).then_inc(p.sems["P0G"], 16)
                p.bump("P0G", 64)
                pg_t = p.val("P0G")
                p.add("gpsimd", f_pd)

                def f_pdw(eng, pg_t=pg_t):
                    eng.wait_ge(p.sems["P0G"], pg_t)
                p.add("scalar", f_pdw)

    st_done = p.val("CTX_ST")

    def f_end(eng, st_done=st_done):
        eng.wait_ge(p.sems["CTX_ST"], st_done)
    p.add("sync", f_end)



def plan_wo_phase(p, env, wo, ctx_scr, x1_scr, x2_scr):
    mp = env["main_ps"]
    ctxt = env["ctxt"]
    xs, stg = env["xs_ring"], env["stg_ring"]
    main_rec = env.setdefault("main_rec", {})
    gidx = env.setdefault("gidx", [0])

    for tile in range(NTB):
        t0 = tile * TB
        for cc in range(HC):
            need = p.val("CTL_FREE")

            def fld(eng, cc=cc, t0=t0, need=need):
                if need:
                    eng.wait_ge(p.sems["CTL_FREE"], need)
                eng.dma_start(ctxt[:, cc * TB:(cc + 1) * TB],
                              ctx_scr[cc * 128:(cc + 1) * 128,
                                      t0:t0 + TB]).then_inc(p.sems["CTL_LD"], 16)
            p.dma_marks.add("CTL_LD"); p.bump("CTL_LD", 16)
            p.add("sync", fld)
        ld_all = p.val("CTL_LD")

        for fc in range(HC):
            buf, (lds, ldt), fr = plan_wcol_load(p, env, wo, fc * 128, HID, 2)
            for tb in range(2):
                g = gidx[0]
                gidx[0] += 1
                slot = g % 4
                ps = mp[:, slot * 512:(slot + 1) * 512]
                prev = main_rec.get(slot)
                pe_t = p.bump("PE_G")

                def mk(eng, buf=buf, lds=lds, ldt=ldt, tb=tb, prev=prev,
                       ld_all=ld_all, ps=ps):
                    eng.wait_ge(lds, ldt)
                    eng.wait_ge(p.sems["CTL_LD"], ld_all)
                    if prev is not None:
                        eng.wait_ge(p.sems[prev[0]], prev[1])
                    for cc in range(HC):
                        i = eng.matmul(
                            ps[:, :], buf[:, cc * 128:(cc + 1) * 128],
                            ctxt[:, cc * TB + tb * 512: cc * TB + (tb + 1) * 512],
                            start=(cc == 0), stop=(cc == HC - 1))
                    i.then_inc(p.sems["PE_G"], 1)
                p.add("tensor", mk)

                tt0 = t0 + tb * 512
                slot2, buf2, (lds2, ldt2), fr2 = xs.produce(
                    "sync",
                    lambda eng, dstb, fc=fc, tt0=tt0: eng.dma_start(
                        dstb[:, :], x1_scr[fc * 128:(fc + 1) * 128,
                                           tt0:tt0 + 512]),
                    free_amt=1)
                st_slot = stg.i
                stg.i = (stg.i + 1) % stg.n
                sbuf = stg.slots[st_slot]
                need_free = stg.fr_expect[st_slot]
                ev_t = p.bump("DVE_EVAC")
                rdy = p.bump(stg.ld_n["eng"][st_slot], 1)
                p.dma_marks.add(stg.fr_n[st_slot])
                stg.fr_expect[st_slot] = p.bump(stg.fr_n[st_slot], 16)

                def ev(eng, ps=ps, buf2=buf2, sbuf=sbuf, lds2=lds2, ldt2=ldt2,
                       fr=fr, fr2=fr2, pe_t=pe_t, need_free=need_free,
                       st_slot=st_slot):
                    eng.wait_ge(p.sems["PE_G"], pe_t)
                    eng.wait_ge(lds2, ldt2)
                    if need_free:
                        eng.wait_ge(stg.fr[st_slot], need_free)
                    i = eng.tensor_add(sbuf[:, :], buf2[:, :], ps[:, :])
                    i.then_inc(p.sems["DVE_EVAC"], 1)
                    eng.engine_nop().then_inc(fr, 1)
                    eng.engine_nop().then_inc(fr2, 1)
                    eng.engine_nop().then_inc(stg.ld["eng"][st_slot], 1)
                p.add("vector", ev)
                main_rec[slot] = ("DVE_EVAC", ev_t)

                def fst(eng, sbuf=sbuf, fc=fc, tt0=tt0, st_slot=st_slot, rdy=rdy):
                    eng.wait_ge(stg.ld["eng"][st_slot], rdy)
                    eng.dma_start(x2_scr[fc * 128:(fc + 1) * 128, tt0:tt0 + 512],
                                  sbuf[:, :]).then_inc(stg.fr[st_slot], 16)
                p.add("gpsimd", fst)

        done = p.val("PE_G")

        def f_free(eng, done=done):
            eng.wait_ge(p.sems["PE_G"], done)
            eng.nop().then_inc(p.sems["CTL_FREE"], 1)
        p.bump("CTL_FREE")
        p.add("tensor", f_free)

    _sync_end_wait(p, stg)



def build_program(dbg=(), phases=("A", "B", "C", "D1", "D2")):
    nc = bass.Bass("TRN2", target_bir_lowering=False, debug=False,
                   detect_race_conditions=False)

    def dram_in(name, shape, dt):
        return nc.dram_tensor(name, shape, dt, kind="ExternalInput").ap()

    def dram_scr(name, shape, dt):
        kind = "ExternalOutput" if name in dbg else "Internal"
        return nc.dram_tensor(name, shape, dt, kind=kind).ap()

    x0 = dram_in("x0", [HID, S], F32)
    wq_t = dram_in("wq_t", [HID, HID], F32)
    wk_t = dram_in("wk_t", [NKV * 128, HID], F32)
    wv_t = dram_in("wv_t", [512, INT], F32)
    wo_t = dram_in("wo_t", [HID, HID], F32)
    up_t = dram_in("up_t", [INT, HID], F32)
    down_t = dram_in("down_t", [HID, INT], F32)
    fc1_t = dram_in("fc1_t", [LAT, HID], F32)
    fc2_t = dram_in("fc2_t", [HID, LAT], F32)
    exp_t = dram_in("exp_t", [LAT, LAT], F32)
    gate_t = dram_in("gate_t", [128, 256], F32)
    gb_in = dram_in("gb_in", [128, 64], F32)
    bv_in = dram_in("bv_in", [1, 1024], F32)
    idbf_in = dram_in("idbf_in", [128, 128], BF16)
    idf_in = dram_in("idf_in", [128, 128], F32)
    p28_in = dram_in("p28_in", [8, 28], BF16)

    out = nc.dram_tensor("out", [HID, S], F32, kind="ExternalOutput").ap()

    x1_scr = dram_scr("x1_scr", [HID, S], F32)
    x2_scr = dram_scr("x2_scr", [HID, S], F32)
    q_scr = dram_scr("q_scr", [HID, S], BF16)
    ctx_scr = dram_scr("ctx_scr", [HID, S], BF16)
    up_bf = dram_scr("up_bf", [INT, HID], BF16)
    down_bf = dram_scr("down_bf", [HID, INT], BF16)
    fc1_bf = dram_scr("fc1_bf", [LAT, HID], BF16)
    fc2_bf = dram_scr("fc2_bf", [HID, LAT], BF16)
    exp_bf = dram_scr("exp_bf", [LAT, LAT], BF16)

    with contextlib.ExitStack() as master:
        p = Plan(nc, master)
        for s in SEM_NAMES:
            p.sem(s)

        sb = lambda name, shape, dt: master.enter_context(
            nc.sbuf_tensor(name, shape, dt))
        gate_sb = sb("gate_sb", [128, 256], BF16)
        gb2 = sb("gb2", [128, 64], F32)
        bvb_sb = sb("bvb_sb", [128, 1024], F32)
        id_bf = sb("id_bf", [128, 128], BF16)
        id_f32 = sb("id_f32", [128, 128], F32)
        p28_sb = sb("p28_sb", [8, 28], BF16)
        ones_bf = sb("ones_bf", [128, 1], BF16)
        ones_row = sb("ones_row", [1, 128], F32)
        rows_sb = sb("rows_sb", [1, 2048], F32)
        fence_sb = sb("fence_sb", [1, 8], F32)
        p.fence_sb = fence_sb

        xs_glob = Ring(p, "GX", [sb(f"gxs{i}", [128, 512], F32)
                                 for i in range(6)], queues=("sync",))
        stg_glob = Ring(p, "GS", [sb(f"gst{i}", [128, 512], F32)
                                  for i in range(4)], queues=("eng",))
        main_ps = master.enter_context(nc.psum_tensor("main_ps", [128, 2048], F32))
        p4 = master.enter_context(nc.psum_tensor("p4", [128, 512], F32))
        p5 = master.enter_context(nc.psum_tensor("p5", [128, 512], F32))
        p6 = master.enter_context(nc.psum_tensor("p6", [128, 512], F32))
        p7 = master.enter_context(nc.psum_tensor("p7", [128, 512], F32))

        base_env = dict(main_ps=main_ps, p4=p4, p5=p5, p6=p6, p7=p7,
                        ones_bf=ones_bf, ones_row=ones_row, rows_sb=rows_sb,
                        gate_sb=gate_sb, gb2=gb2, id_bf=id_bf, id_f32=id_f32,
                        p28_sb=p28_sb)

        def f_p0_g(eng):
            eng.dma_start(gate_sb[:, :], gate_t[:, :]).then_inc(p.sems["P0G"], 16)
        p.dma_marks.add("P0G"); p0g = p.bump("P0G", 16)
        p.add("gpsimd", f_p0_g)

        def f_p0_s(eng):
            eng.dma_start(gb2[:, :], gb_in[:, :]).then_inc(p.sems["P0"], 16)
            eng.dma_start(id_bf[:, :], idbf_in[:, :]).then_inc(p.sems["P0"], 16)
            eng.dma_start(id_f32[:, :], idf_in[:, :]).then_inc(p.sems["P0"], 16)
            eng.dma_start(p28_sb[:, :], p28_in[:, :]).then_inc(p.sems["P0"], 16)
            eng.dma_start(rows_sb[0:1, 0:1024], bv_in[0:1, :]).then_inc(
                p.sems["P0"], 16)
        p.bump("P0", 16 * 5)
        p0_dma = p.val("P0")
        p.add("sync", f_p0_s)

        def f_p0_v(eng, p0_dma=p0_dma, p0g=p0g):
            eng.wait_ge(p.sems["P0"], p0_dma)
            eng.wait_ge(p.sems["P0G"], p0g)
            eng.memset(fence_sb[0:1, :], 1.0)
            eng.memset(ones_bf[:, :], 1.0)
            eng.memset(ones_row[0:1, :], 1.0)
            eng.tensor_scalar(gb2[:, :], gb2[:, :], 0.5, None,
                              op0=ALU.mult).then_inc(p.sems["DVE_ROWS"], 1)
        dve_p0 = p.bump("DVE_ROWS")
        p.add("vector", f_p0_v)

        def f_p0_t(eng, dve_p0=dve_p0):
            eng.wait_ge(p.sems["DVE_ROWS"], dve_p0)
            eng.matmul(p6[:, :], ones_row[0:1, :], rows_sb[0:1, 0:512],
                       start=True, stop=True)
            eng.matmul(p7[:, :], ones_row[0:1, :], rows_sb[0:1, 512:1024],
                       start=True, stop=True).then_inc(p.sems["PE_BC"], 1)
        pe_p0 = p.bump("PE_BC")
        p.add("tensor", f_p0_t)

        def f_p0_v2(eng, pe_p0=pe_p0):
            eng.wait_ge(p.sems["PE_BC"], pe_p0)
            eng.tensor_copy(bvb_sb[:, 0:512], p6[:, :])
            eng.tensor_copy(bvb_sb[:, 512:1024], p7[:, :]).then_inc(
                p.sems["BC_FREE"], 1)
        p.bump("BC_FREE")
        p.add("vector", f_p0_v2)
        p.barrier()
        p.flush()

        if "A" in phases:
          with contextlib.ExitStack() as sa:
            sbA = lambda name, shape, dt: sa.enter_context(
                nc.sbuf_tensor(name, shape, dt))
            envA = dict(base_env)
            envA["y1s"] = sbA("y1s", [128, 16384], BF16)
            envA["hx"] = sbA("hx", [128, 32768], BF16)
            envA["moe"] = sbA("moe", [128, 8192], BF16)
            awpool = sbA("awpool", [128, 20480], BF16)
            envA["ring_sm"] = Ring(
                p, "AWS", [awpool[:, i * 4096:(i + 1) * 4096] for i in range(5)],
                queues=("gpsimd", "sync"))
            envA["ring_dn"] = Ring(
                p, "AWD", [awpool[:, j * 10240:(j + 1) * 10240] for j in range(2)],
                queues=("gpsimd", "sync"))
            envA["xs_ring"] = xs_glob
            envA["stg_ring"] = stg_glob
            envA["xbf"] = [sbA("axbf0", [128, 512], BF16),
                           sbA("axbf1", [128, 512], BF16)]
            envA["sqs"] = [sbA("asqs0", [128, 512], BF16),
                          sbA("asqs1", [128, 512], BF16)]
            envA["ztmp"] = sbA("aztmp", [128, 512], F32)
            envA["logits_bf"] = sbA("alogits", [8, 512], BF16)
            envA["pairs_bf"] = sbA("apairs", [28, 512], BF16)
            envA["rwcols"] = sbA("arwcols", [128, 4], F32)
            envA["rwrow_sb"] = sbA("arwrow", [1, 512], F32)
            envA["rwb_sb"] = sbA("arwb", [128, 512], F32)

            envA["snap"] = dram_scr("snap0", [HID, TA], F32) if "snap0" in dbg else None

            def wkeysA(t):
                if t == 0:
                    return {"up": ("cast_spill", up_t, up_bf),
                            "down": ("cast_spill", down_t, down_bf),
                            "fc1": ("cast_spill", fc1_t, fc1_bf),
                            "fc2": ("cast_spill", fc2_t, fc2_bf),
                            "expert": ("cast_spill", exp_t, exp_bf)}
                return {"up": ("bf", up_bf), "down": ("bf", down_bf),
                        "fc1": ("bf", fc1_bf), "fc2": ("bf", fc2_bf),
                        "expert": ("bf", exp_bf)}
            plan_ffn_phase(p, envA, x0, x1_scr, wkeysA)
            p.barrier()
            p.flush()

        if "B" in phases or "C" in phases:
          with contextlib.ExitStack() as sbc:
            sbBC = lambda name, shape, dt: sbc.enter_context(
                nc.sbuf_tensor(name, shape, dt))
            K_sb = sbBC("K_sb", [128, NKV * S], BF16)
            V_sb = sbBC("V_sb", [128, 16 * 1024], BF16)

            if "B" in phases:
              with contextlib.ExitStack() as sB:
                sbB = lambda name, shape, dt: sB.enter_context(
                    nc.sbuf_tensor(name, shape, dt))
                envB = dict(base_env)
                envB["y2"] = sbB("y2", [128, HC * TB], BF16)
                envB["w_ring"] = Ring(p, "BW", [sbB("bwb0", [128, 8192], BF16),
                                                sbB("bwb1", [128, 8192], BF16)],
                                      queues=("gpsimd",))
                envB["xs_ring"] = xs_glob
                envB["stg_ring"] = Ring(p, "BS", [sbB("bst0", [128, 512], BF16),
                                                  sbB("bst1", [128, 512], BF16)],
                                        queues=("eng",))
                envB["xbf"] = [sbB("bxbf0", [128, 512], BF16),
                               sbB("bxbf1", [128, 512], BF16)]
                envB["sqs"] = [sbB("bsqs0", [128, 512], BF16),
                              sbB("bsqs1", [128, 512], BF16)]
                plan_qkv_phase(p, envB, x1_scr, ("cast", wq_t), ("cast", wk_t),
                               ("cast", wv_t), q_scr, K_sb, V_sb, bvb_sb)
                p.barrier()
                p.flush()

            if "kv_dbg" in dbg:
                k_dbg = nc.dram_tensor("k_dbg", [128, NKV * S], F32,
                                       kind="ExternalOutput").ap()
                v_dbg = nc.dram_tensor("v_dbg", [128, 16 * 1024], F32,
                                       kind="ExternalOutput").ap()
                q_dbg = nc.dram_tensor("q_dbg", [HID, S], F32,
                                       kind="ExternalOutput").ap()

                def f_kv(eng):
                    eng.dma_start(k_dbg[:, :], K_sb[:, :]).then_inc(
                        p.sems["P0G"], 16)
                    eng.dma_start(v_dbg[:, :], V_sb[:, :]).then_inc(
                        p.sems["P0G"], 16)
                    for hc in range(HC):
                        eng.dma_start(q_dbg[hc * 128:(hc + 1) * 128, :],
                                      q_scr[hc * 128:(hc + 1) * 128, :]).then_inc(
                                          p.sems["P0G"], 16)
                p.bump("P0G", 16 * (2 + HC))
                p.add("gpsimd", f_kv)
                p.barrier()
                p.flush()

            if "C" in phases:
              with contextlib.ExitStack() as sC:
                sbC = lambda name, shape, dt: sC.enter_context(
                    nc.sbuf_tensor(name, shape, dt))
                envC = dict(base_env)
                envC["qh_ring"] = Ring(p, "CQ", [sbC("cqh0", [128, 512], BF16),
                                                 sbC("cqh1", [128, 512], BF16)])
                envC["P_sb"] = sbC("cP", [128, 4096], BF16)
                envC["PT_sb"] = sbC("cPT", [128, 8192], BF16)
                envC["ctx_sb"] = sbC("cctx", [128, 16384], BF16)
                envC["nmax"] = sbC("cnmax", [128, 16], F32)
                envC["Dr"] = sbC("cDr", [128, 256], BF16)
                envC["recip"] = sbC("crecip", [128, 2], F32)
                if "attn_dbg" in dbg:
                    envC["nm_dbg"] = nc.dram_tensor(
                        "nm_dbg", [128, 18], F32, kind="ExternalOutput").ap()
                    envC["attn_dbg"] = (
                        nc.dram_tensor("p_dbg", [128, 2048], F32,
                                       kind="ExternalOutput").ap(),
                        nc.dram_tensor("pt_dbg", [128, 8192], F32,
                                       kind="ExternalOutput").ap())
                plan_attn_phase(p, envC, q_scr, K_sb, V_sb, ctx_scr)
                p.barrier()
                p.flush()

        if "ctx_dbg" in dbg:
            ctx_dbg = nc.dram_tensor("ctx_dbg", [HID, S], F32,
                                     kind="ExternalOutput").ap()

            def f_cd(eng):
                for hc in range(HC):
                    eng.dma_start(ctx_dbg[hc * 128:(hc + 1) * 128, :],
                                  ctx_scr[hc * 128:(hc + 1) * 128, :]).then_inc(
                                      p.sems["P0G"], 16)
            p.bump("P0G", 16 * HC)
            p.add("gpsimd", f_cd)
            p.barrier()
            p.flush()

        if "D1" in phases:
          with contextlib.ExitStack() as sD:
            sbD = lambda name, shape, dt: sD.enter_context(
                nc.sbuf_tensor(name, shape, dt))
            envD = dict(base_env)
            envD["ctxt"] = sbD("dctxt", [128, HC * TB], BF16)
            envD["w_ring"] = Ring(p, "DW", [sbD("dwb0", [128, 8192], BF16),
                                            sbD("dwb1", [128, 8192], BF16)],
                                  queues=("gpsimd",))
            envD["xs_ring"] = xs_glob
            envD["stg_ring"] = stg_glob
            plan_wo_phase(p, envD, ("cast", wo_t), ctx_scr, x1_scr, x2_scr)
            p.barrier()
            p.flush()

        if "D2" in phases:
          with contextlib.ExitStack() as sE:
            sbE = lambda name, shape, dt: sE.enter_context(
                nc.sbuf_tensor(name, shape, dt))
            envE = dict(base_env)
            envE["y1s"] = sbE("ey1s", [128, 16384], BF16)
            envE["hx"] = sbE("ehx", [128, 32768], BF16)
            envE["moe"] = sbE("emoe", [128, 8192], BF16)
            ewpool = sbE("ewpool", [128, 20480], BF16)
            envE["ring_sm"] = Ring(
                p, "EWS", [ewpool[:, i * 4096:(i + 1) * 4096] for i in range(5)],
                queues=("sync",))
            envE["ring_dn"] = Ring(
                p, "EWD", [ewpool[:, j * 10240:(j + 1) * 10240] for j in range(2)],
                queues=("sync",))
            envE["xs_ring"] = xs_glob
            envE["stg_ring"] = stg_glob
            envE["xbf"] = [sbE("exbf0", [128, 512], BF16),
                           sbE("exbf1", [128, 512], BF16)]
            envE["sqs"] = [sbE("esqs0", [128, 512], BF16),
                          sbE("esqs1", [128, 512], BF16)]
            envE["ztmp"] = sbE("eztmp", [128, 512], F32)
            envE["logits_bf"] = sbE("elogits", [8, 512], BF16)
            envE["pairs_bf"] = sbE("epairs", [28, 512], BF16)
            envE["rwcols"] = sbE("erwcols", [128, 4], F32)
            envE["rwrow_sb"] = sbE("erwrow", [1, 512], F32)
            envE["rwb_sb"] = sbE("erwb", [128, 512], F32)

            def wkeysE(t):
                return {"up": ("bf", up_bf), "down": ("bf", down_bf),
                        "fc1": ("bf", fc1_bf), "fc2": ("bf", fc2_bf),
                        "expert": ("bf", exp_bf)}
            plan_ffn_phase(p, envE, x2_scr, out, wkeysE)
            p.barrier()
            p.flush()

    return nc



def _prep_inputs(kw):
    bf = ml_dtypes.bfloat16
    w = {}
    w["wq_t"] = tile_w(np.ascontiguousarray(kw["L0_attn_Wq"].T))
    w["wk_t"] = tile_w(np.ascontiguousarray(kw["L0_attn_Wk"].T))
    w["wv_t"] = tile_wv(np.ascontiguousarray(kw["L0_attn_Wv"].T))
    w["wo_t"] = tile_w(np.ascontiguousarray(kw["L0_attn_Wo"].T))
    w["up_t"] = tile_w(kw["L0_ffn_up_w"])
    w["down_t"] = tile_w(kw["L0_ffn_down_w"])
    w["fc1_t"] = tile_w(kw["L0_ffn_fc1_w"])
    w["fc2_t"] = tile_w(kw["L0_ffn_fc2_w"])
    w["exp_t"] = tile_w(kw["L0_ffn_expert_w"])
    w["gate_t"] = np.ascontiguousarray(
        kw["L0_ffn_gate_w"].reshape(32, 128, 8).transpose(1, 0, 2).reshape(128, 256))
    gb = np.concatenate([kw["L0_ffn_ln_gamma"].reshape(32, 128).T,
                         kw["L0_ffn_ln_beta"].reshape(32, 128).T], axis=1)
    w["gb_in"] = np.ascontiguousarray(gb.astype(np.float32))
    w["bv_in"] = np.ascontiguousarray(
        kw["L0_attn_bv"].reshape(1, 1024).astype(np.float32))
    w["idbf_in"] = np.eye(128, dtype=bf)
    w["idf_in"] = np.eye(128, dtype=np.float32)
    w["p28_in"] = _pairs_mat().astype(bf)
    for k in list(w.keys()):
        if w[k].dtype == np.float64:
            w[k] = w[k].astype(np.float32)
    return w


def kernel(**kw):
    x = np.asarray(kw["L0_attn_hidden_states"], dtype=np.float32)
    w = _prep_inputs({k: np.asarray(v) for k, v in kw.items()
                      if k != "L0_attn_hidden_states"})
    nc = build_program()
    in_maps = []
    for b in range(8):
        m = dict(w)
        m["x0"] = np.ascontiguousarray(x[b].T)
        in_maps.append(m)
    res = run_bass_kernel_spmd(nc, in_maps, core_ids=list(range(8)))
    outs = [np.ascontiguousarray(r["out"].T) for r in res.results]
    return np.stack(outs, axis=0).astype(np.float32)


# version 13 (final)
# speedup vs baseline: 1.0391x; 1.0391x over previous
import contextlib
import math

import numpy as np
import ml_dtypes

import concourse.bass as bass
import concourse.mybir as mybir
from concourse.bass_utils import run_bass_kernel_spmd

F32 = mybir.dt.float32
BF16 = mybir.dt.bfloat16
AF = mybir.ActivationFunctionType
ALU = mybir.AluOpType
AX = mybir.AxisListType

HID = 4096
S = 2048
NQ, NKV, D = 32, 8, 128
INT = 8192
LAT = 2048
HC = HID // 128
IC = INT // 128
LC = LAT // 128
EPS = 1e-5
QSCALE = 1.0 / math.sqrt(D)
TA = 512
NTA = S // TA
TB = 1024
NTB = S // TB

ENGINES = ("tensor", "vector", "scalar", "gpsimd", "sync")

SEM_NAMES = [
    "BAR", "P0", "DVE_CAST", "PE_STATF", "DVE_ROWS", "ACT_ROWS", "PE_BC",
    "BC_FREE", "DVE_Y", "PE_G", "DVE_EVAC", "ACT_EVAC", "DVE_H", "DVE_SBF",
    "PE_SC", "DVE_MAX", "ACT_P", "DVE_RCP", "GP_PN", "PE_PT2", "PE_CTX2",
    "DVE_CTX", "CTX_ST", "CTL_LD", "CTL_FREE", "P0G",
]


class Plan:

    def __init__(self, nc, es):
        self.nc = nc
        self.q = {e: [] for e in ENGINES}
        self.sems = {}
        self.cnt = {}
        self.es = es
        self.dma_marks = set()
        self.fence_sb = None
        self.nbar = 0

    def sem(self, name):
        if name not in self.sems:
            self.sems[name] = self.es.enter_context(self.nc.semaphore(name))
            self.cnt[name] = 0
        return self.sems[name]

    def bump(self, name, n=1):
        self.sem(name)
        self.cnt[name] += n
        return self.cnt[name]

    def val(self, name):
        self.sem(name)
        return self.cnt[name]

    def add(self, eng, fn):
        self.q[eng].append(fn)

    def flush(self):
        nc = self.nc
        with nc.Block() as blk:
            for e in ENGINES:
                fns = self.q[e]
                if not fns:
                    continue
                getattr(blk, e)(lambda eng, fns=fns: [f(eng) for f in fns])
        self.q = {e: [] for e in ENGINES}

    def barrier(self):
        waits = [(n, self.cnt[n]) for n in sorted(self.dma_marks)
                 if self.cnt[n] > 0]

        def fw(eng, waits=tuple(waits)):
            for n, v in waits:
                eng.wait_ge(self.sems[n], v)
        self.add("sync", fw)
        self.nbar += 1
        bar = self.sem("BAR")
        tgt = self.bump("BAR", len(ENGINES))
        for e in ENGINES:
            def f(eng, bar=bar, tgt=tgt, e=e):
                if e in ("vector", "gpsimd"):
                    eng.engine_nop().then_inc(bar, 1)
                elif e == "scalar" and self.fence_sb is not None and self.nbar > 1:
                    eng.mul(self.fence_sb[0:1, 0:1], self.fence_sb[0:1, 0:1],
                            1.0).then_inc(bar, 1)
                else:
                    eng.nop().then_inc(bar, 1)
                eng.wait_ge(bar, tgt)
            self.add(e, f)


class Ring:

    def __init__(self, p, name, slots, queues=("sync",)):
        self.p = p
        self.slots = slots
        self.n = len(slots)
        self.i = 0
        self.ld = {}
        self.ld_n = {}
        for q in queues:
            if q == "eng":
                self.ld[q] = [p.sem(f"{name}_{q}_ld")] * self.n
                self.ld_n[q] = [f"{name}_{q}_ld"] * self.n
            else:
                self.ld[q] = [p.sem(f"{name}_{q}_ld{i}") for i in range(self.n)]
                self.ld_n[q] = [f"{name}_{q}_ld{i}" for i in range(self.n)]
        self.fr = [p.sem(f"{name}_fr{i}") for i in range(self.n)]
        self.fr_n = [f"{name}_fr{i}" for i in range(self.n)]
        self.fr_expect = [0] * self.n

    def produce(self, queue, dma_fn, free_amt, n_dma=1, extra_waits=()):
        p = self.p
        slot = self.i
        self.i = (self.i + 1) % self.n
        need_free = self.fr_expect[slot]
        buf = self.slots[slot]
        ld_sem, fr_sem = self.ld[queue][slot], self.fr[slot]

        def f(eng, buf=buf, need_free=need_free, dma_fn=dma_fn, fr_sem=fr_sem,
              ld_sem=ld_sem, extra_waits=tuple(extra_waits)):
            for name, v in extra_waits:
                if v:
                    eng.wait_ge(p.sems[name], v)
            if need_free:
                eng.wait_ge(fr_sem, need_free)
            instrs = dma_fn(eng, buf)
            if not isinstance(instrs, (list, tuple)):
                instrs = [instrs]
            for i in instrs:
                i.then_inc(ld_sem, 16)
        p.add(queue, f)
        p.dma_marks.add(self.ld_n[queue][slot])
        tgt = p.bump(self.ld_n[queue][slot], 16 * n_dma)
        self.fr_expect[slot] = p.bump(self.fr_n[slot], free_amt)
        return slot, buf, (ld_sem, tgt), fr_sem

    def fr_waits(self):
        return [(self.fr_n[i], self.fr_expect[i]) for i in range(self.n)]


def tile_w(W):
    IN, OUT = W.shape
    t = W.reshape(IN // 128, 128, OUT // 128, 128).transpose(2, 1, 0, 3)
    return np.ascontiguousarray(t.reshape(OUT, IN))


def tile_wv(WvT):
    t = WvT.reshape(32, 128, 4, 256).transpose(2, 1, 0, 3)
    return np.ascontiguousarray(t.reshape(512, 8192))


def _pairs_mat():
    cols = []
    for i in range(8):
        for j in range(i + 1, 8):
            c = np.zeros(8, np.float32)
            c[i] = 1.0
            c[j] = 1.0
            cols.append(c)
    return np.stack(cols, axis=1)



def plan_ln_pass(p, env, src, t0, y_dst):
    p4, p5, p6, p7 = env["p4"], env["p5"], env["p6"], env["p7"]
    xs = env["xs_ring"]
    xbf, sqs = env["xbf"], env["sqs"]
    ones_bf, ones_row = env["ones_bf"], env["ones_row"]
    rows = env["rows_sb"]
    T = 512

    stat_base = p.val("PE_STATF")
    for hc in range(HC):
        slot, buf, (lds, ldt), fr = xs.produce(
            "sync",
            lambda eng, dst, hc=hc: eng.dma_start(
                dst[:, :], src[hc * 128:(hc + 1) * 128, t0:t0 + T]),
            free_amt=1)
        xb = xbf[hc % 2]
        sq = sqs[hc % 2]
        pe_need = stat_base + hc - 1 if hc >= 2 else 0

        def fv(eng, buf=buf, xb=xb, sq=sq, lds=lds, ldt=ldt, fr=fr, pe_need=pe_need):
            eng.wait_ge(lds, ldt)
            if pe_need:
                eng.wait_ge(p.sems["PE_STATF"], pe_need)
            eng.tensor_copy(xb[:, :], buf[:, :]).then_inc(fr, 1)
            eng.tensor_mul(sq[:, :], xb[:, :], xb[:, :]).then_inc(
                p.sems["DVE_CAST"], 1)
        dve_tgt = p.bump("DVE_CAST")
        p.add("vector", fv)

        def ft(eng, xb=xb, sq=sq, hc=hc, dve_tgt=dve_tgt):
            eng.wait_ge(p.sems["DVE_CAST"], dve_tgt)
            eng.matmul(p4[0:1, :], ones_bf[:, 0:1], xb[:, :],
                       start=(hc == 0), stop=(hc == HC - 1))
            eng.matmul(p5[0:1, :], ones_bf[:, 0:1], sq[:, :],
                       start=(hc == 0), stop=(hc == HC - 1)).then_inc(
                           p.sems["PE_STATF"], 1)
        p.bump("PE_STATF")
        p.add("tensor", ft)

    pe_stat = p.val("PE_STATF")

    def fv_rows(eng, pe_stat=pe_stat):
        eng.wait_ge(p.sems["PE_STATF"], pe_stat)
        eng.tensor_scalar(rows[0:1, 0:512], p4[0:1, :], 1.0 / HID, None, op0=ALU.mult)
        eng.tensor_scalar(rows[0:1, 512:1024], p5[0:1, :], 1.0 / HID, None, op0=ALU.mult)
        eng.tensor_mul(rows[0:1, 1024:1536], rows[0:1, 0:512], rows[0:1, 0:512])
        eng.tensor_sub(rows[0:1, 1024:1536], rows[0:1, 512:1024],
                       rows[0:1, 1024:1536])
        eng.tensor_scalar(rows[0:1, 1024:1536], rows[0:1, 1024:1536], EPS, None,
                          op0=ALU.add).then_inc(p.sems["DVE_ROWS"], 1)
    p.bump("DVE_ROWS")
    p.add("vector", fv_rows)

    act_t = p.val("DVE_ROWS")

    def fs_rows(eng, act_t=act_t):
        eng.wait_ge(p.sems["DVE_ROWS"], act_t)
        eng.activation(rows[0:1, 1536:2048], rows[0:1, 1024:1536],
                       AF.Sqrt).then_inc(p.sems["ACT_ROWS"], 1)
    p.bump("ACT_ROWS")
    p.add("scalar", fs_rows)

    act_v = p.val("ACT_ROWS")

    def fv_recip(eng, act_v=act_v):
        eng.wait_ge(p.sems["ACT_ROWS"], act_v)
        eng.reciprocal(rows[0:1, 512:1024], rows[0:1, 1536:2048]).then_inc(
            p.sems["DVE_ROWS"], 1)
    p.bump("DVE_ROWS")
    p.add("vector", fv_recip)

    dve_rows = p.val("DVE_ROWS")
    bc_need = p.val("BC_FREE")

    def ft_bc(eng, dve_rows=dve_rows, bc_need=bc_need):
        eng.wait_ge(p.sems["DVE_ROWS"], dve_rows)
        if bc_need:
            eng.wait_ge(p.sems["BC_FREE"], bc_need)
        eng.matmul(p6[:, :], ones_row[0:1, :], rows[0:1, 0:512], start=True, stop=True)
        eng.matmul(p7[:, :], ones_row[0:1, :], rows[0:1, 512:1024],
                   start=True, stop=True).then_inc(p.sems["PE_BC"], 1)
    pe_bc = p.bump("PE_BC")
    p.add("tensor", ft_bc)

    for hc in range(HC):
        slot, buf, (lds, ldt), fr = xs.produce(
            "sync",
            lambda eng, dst, hc=hc: eng.dma_start(
                dst[:, :], src[hc * 128:(hc + 1) * 128, t0:t0 + T]),
            free_amt=1)
        ydst = y_dst(hc)
        last = (hc == HC - 1)

        def fv(eng, buf=buf, ydst=ydst, lds=lds, ldt=ldt, fr=fr, pe_bc=pe_bc,
               last=last):
            eng.wait_ge(lds, ldt)
            eng.wait_ge(p.sems["PE_BC"], pe_bc)
            eng.tensor_sub(buf[:, :], buf[:, :], p6[:, :])
            i = eng.tensor_mul(ydst, buf[:, :], p7[:, :])
            i.then_inc(fr, 1)
            if last:
                eng.engine_nop().then_inc(p.sems["DVE_Y"], 1)
                eng.engine_nop().then_inc(p.sems["BC_FREE"], 1)
        p.add("vector", fv)
    p.bump("DVE_Y")
    p.bump("BC_FREE")


def plan_wcol_load(p, env, key, rows0, ncols, frees_pe, ring=None,
                   extra_waits=()):
    mode, src = key[0], key[1]
    wr = ring if ring is not None else env["w_ring"]
    if mode == "bf":
        slot, buf, ld, fr = wr.produce(
            "sync",
            lambda eng, dst: eng.dma_start(
                dst[:, 0:ncols], src[rows0:rows0 + 128, 0:ncols]),
            free_amt=frees_pe, extra_waits=extra_waits)
    else:
        spill_amt = 16 if mode == "cast_spill" else 0
        slot, buf, ld, fr = wr.produce(
            "gpsimd",
            lambda eng, dst: eng.dma_start(
                dst[:, 0:ncols], src[rows0:rows0 + 128, 0:ncols]),
            free_amt=frees_pe + spill_amt, extra_waits=extra_waits)
        if mode == "cast_spill":
            spill = key[2]
            lds, ldt = ld

            def fsp(eng, buf=buf, lds=lds, ldt=ldt, fr=fr, rows0=rows0,
                    ncols=ncols, spill=spill):
                eng.wait_ge(lds, ldt)
                eng.dma_start(spill[rows0:rows0 + 128, 0:ncols],
                              buf[:, 0:ncols]).then_inc(fr, 16)
            p.add("sync", fsp)
            p.dma_marks.add(wr.fr_n[slot])
    return buf, ld, fr


def _sync_end_wait(p, stg):
    waits = [(stg.fr_n[i], p.val(stg.fr_n[i])) for i in range(stg.n)]

    def f_end(eng, waits=tuple(waits)):
        for name, v in waits:
            eng.wait_ge(p.sems[name], v)
    p.add("sync", f_end)



def plan_ffn_phase(p, env, src, dst, wkeys_fn):
    mp = env["main_ps"]
    p4, p5, p6, p7 = env["p4"], env["p5"], env["p6"], env["p7"]
    ones_bf, ones_row = env["ones_bf"], env["ones_row"]
    rows = env["rows_sb"]
    y1s, hx, moe = env["y1s"], env["hx"], env["moe"]
    xs, stg = env["xs_ring"], env["stg_ring"]
    ztmp, sqs = env["ztmp"], env["sqs"]
    logits_bf, pairs_bf = env["logits_bf"], env["pairs_bf"]
    rwcols, rwrow_sb, rwb_sb = env["rwcols"], env["rwrow_sb"], env["rwb_sb"]
    gate_sb, p28 = env["gate_sb"], env["p28_sb"]
    id_bf, id_f32, gb2 = env["id_bf"], env["id_f32"], env["gb2"]
    main_rec = env.setdefault("main_rec", {})
    gidx = env.setdefault("gidx", [0])

    def pe_group(emit_mms, evac_engine, emit_evac, frees=()):
        g = gidx[0]
        gidx[0] += 1
        slot = g % 4
        ps = mp[:, slot * 512:(slot + 1) * 512]
        prev = main_rec.get(slot)
        pe_t = p.bump("PE_G")

        def ft(eng, prev=prev, ps=ps):
            if prev is not None:
                eng.wait_ge(p.sems[prev[0]], prev[1])
            emit_mms(eng, ps).then_inc(p.sems["PE_G"], 1)
        p.add("tensor", ft)
        sem_name = {"vector": "DVE_EVAC", "scalar": "ACT_EVAC"}[evac_engine]
        ev_t = p.bump(sem_name)

        def fe(eng, pe_t=pe_t, ps=ps, frees=tuple(frees)):
            eng.wait_ge(p.sems["PE_G"], pe_t)
            emit_evac(eng, ps).then_inc(p.sems[sem_name], 1)
            for sem, amt in frees:
                eng.nop().then_inc(sem, amt)
        p.add(evac_engine, fe)
        main_rec[slot] = (sem_name, ev_t)
        return ps

    for t in range(NTA):
        t0 = t * TA
        wkeys = wkeys_fn(t)
        plan_ln_pass(p, env, src, t0, lambda hc: y1s[:, hc * 512:(hc + 1) * 512])
        dve_y = p.val("DVE_Y")
        if env.get("stop_stage") == "ln":
            continue

        def mk_gate(eng, ps, dve_y=dve_y):
            eng.wait_ge(p.sems["DVE_Y"], dve_y)
            for hc in range(HC):
                i = eng.matmul(ps[0:8, :], gate_sb[:, hc * 8:(hc + 1) * 8],
                               y1s[:, hc * 512:(hc + 1) * 512],
                               start=(hc == 0), stop=(hc == HC - 1))
            return i

        def ev_gate(eng, ps):
            return eng.tensor_copy(logits_bf[0:8, :], ps[0:8, :])
        pe_group(mk_gate, "vector", ev_gate)
        dve_lg = p.val("DVE_EVAC")
        if env.get("stop_stage") == "g1":
            continue

        def mk_pairs(eng, ps, dve_lg=dve_lg):
            eng.wait_ge(p.sems["DVE_EVAC"], dve_lg)
            return eng.matmul(ps[0:28, :], p28[0:8, :], logits_bf[0:8, :],
                              start=True, stop=True)

        def ev_pairs(eng, ps):
            return eng.tensor_copy(pairs_bf[0:28, :], ps[0:28, :])
        pe_group(mk_pairs, "vector", ev_pairs)
        dve_pr = p.val("DVE_EVAC")
        if env.get("stop_stage") == "g2":
            continue

        def mk_rwtp(eng, ps, dve_pr=dve_pr):
            eng.wait_ge(p.sems["DVE_EVAC"], dve_pr)
            for c in range(4):
                i = eng.matmul(ps[:, c * 32:c * 32 + 28],
                               pairs_bf[0:28, c * 128:(c + 1) * 128],
                               id_bf[0:28, 0:28], start=True, stop=True)
            return i

        def ev_rwtp(eng, ps):
            for c in range(4):
                i = eng.reduce_max(rwcols[:, c:c + 1], ps[:, c * 32:c * 32 + 28],
                                   axis=AX.X)
            return i
        pe_group(mk_rwtp, "vector", ev_rwtp)
        dve_rw = p.val("DVE_EVAC")
        if env.get("stop_stage") == "g3":
            continue

        def mk_rwrow(eng, ps, dve_rw=dve_rw):
            eng.wait_ge(p.sems["DVE_EVAC"], dve_rw)
            for c in range(4):
                i = eng.matmul(ps[0:1, c * 128:(c + 1) * 128], rwcols[:, c:c + 1],
                               id_f32[:, :], start=True, stop=True)
            return i

        def ev_rwrow(eng, ps):
            return eng.tensor_copy(rwrow_sb[0:1, :], ps[0:1, :])
        pe_group(mk_rwrow, "vector", ev_rwrow)
        dve_rr = p.val("DVE_EVAC")
        if env.get("stop_stage") == "g4":
            continue
        bc_need = p.val("BC_FREE")

        def ft_rwb(eng, dve_rr=dve_rr, bc_need=bc_need):
            eng.wait_ge(p.sems["DVE_EVAC"], dve_rr)
            if bc_need:
                eng.wait_ge(p.sems["BC_FREE"], bc_need)
            eng.matmul(p6[:, :], ones_row[0:1, :], rwrow_sb[0:1, :],
                       start=True, stop=True).then_inc(p.sems["PE_BC"], 1)
        pe_bc_rw = p.bump("PE_BC")
        p.add("tensor", ft_rwb)

        def fv_rwb(eng, pe_bc_rw=pe_bc_rw):
            eng.wait_ge(p.sems["PE_BC"], pe_bc_rw)
            eng.tensor_copy(rwb_sb[:, :], p6[:, :]).then_inc(p.sems["BC_FREE"], 1)
        p.bump("BC_FREE")
        p.add("vector", fv_rwb)

        if env.get("stop_stage") == "gate":
            continue
        for lc in range(LC):
            buf, (lds, ldt), fr = plan_wcol_load(
                p, env, wkeys["fc1"], lc * 128, HID, 1, ring=env["ring_sm"],
                extra_waits=(env["ring_dn"].fr_waits() if lc == 0 else ()))

            def mk(eng, ps, buf=buf, lds=lds, ldt=ldt, dve_y=dve_y):
                eng.wait_ge(lds, ldt)
                eng.wait_ge(p.sems["DVE_Y"], dve_y)
                for hc in range(HC):
                    i = eng.matmul(ps[:, :], buf[:, hc * 128:(hc + 1) * 128],
                                   y1s[:, hc * 512:(hc + 1) * 512],
                                   start=(hc == 0), stop=(hc == HC - 1))
                return i

            def ev(eng, ps, lc=lc):
                return eng.tensor_copy(hx[:, lc * 512:(lc + 1) * 512], ps[:, :])
            pe_group(mk, "vector", ev, frees=[(fr, 1)])
        dve_xlat = p.val("DVE_EVAC")

        for mc in range(LC):
            buf, (lds, ldt), fr = plan_wcol_load(p, env, wkeys["expert"],
                                                 mc * 128, LAT, 1,
                                                 ring=env["ring_sm"])

            def mk(eng, ps, buf=buf, lds=lds, ldt=ldt, dve_xlat=dve_xlat):
                eng.wait_ge(lds, ldt)
                eng.wait_ge(p.sems["DVE_EVAC"], dve_xlat)
                for lc in range(LC):
                    i = eng.matmul(ps[:, :], buf[:, lc * 128:(lc + 1) * 128],
                                   hx[:, lc * 512:(lc + 1) * 512],
                                   start=(lc == 0), stop=(lc == LC - 1))
                return i

            def ev(eng, ps, mc=mc):
                return eng.tensor_mul(moe[:, mc * 512:(mc + 1) * 512],
                                      rwb_sb[:, :], ps[:, :])
            pe_group(mk, "vector", ev, frees=[(fr, 1)])

        if env.get("stop_stage") == "lat":
            continue
        for fc in range(IC):
            buf, (lds, ldt), fr = plan_wcol_load(p, env, wkeys["up"], fc * 128,
                                                 HID, 1, ring=env["ring_sm"])

            def mk(eng, ps, buf=buf, lds=lds, ldt=ldt, dve_y=dve_y):
                eng.wait_ge(lds, ldt)
                eng.wait_ge(p.sems["DVE_Y"], dve_y)
                for hc in range(HC):
                    i = eng.matmul(ps[:, :], buf[:, hc * 128:(hc + 1) * 128],
                                   y1s[:, hc * 512:(hc + 1) * 512],
                                   start=(hc == 0), stop=(hc == HC - 1))
                return i

            def ev(eng, ps, fc=fc):
                return eng.activation(hx[:, fc * 512:(fc + 1) * 512], ps[:, :],
                                      AF.Relu)
            pe_group(mk, "scalar", ev, frees=[(fr, 1)])
            act_h = p.val("ACT_EVAC")

            def fv_sq(eng, fc=fc, act_h=act_h):
                eng.wait_ge(p.sems["ACT_EVAC"], act_h)
                sl = hx[:, fc * 512:(fc + 1) * 512]
                eng.tensor_mul(sl, sl, sl).then_inc(p.sems["DVE_H"], 1)
            p.bump("DVE_H")
            p.add("vector", fv_sq)
        dve_h = p.val("DVE_H")

        if env.get("stop_stage") == "up":
            continue
        for gc in range(HC):
            kd, kf = wkeys["down"], wkeys["fc2"]
            mode = kd[0]
            wr = env["ring_dn"]
            extra = env["ring_sm"].fr_waits() if gc == 0 else ()
            spill_amt = 32 if mode == "cast_spill" else 0

            def dma_fn(eng, dst, kd=kd, kf=kf, gc=gc):
                i1 = eng.dma_start(dst[:, 0:INT],
                                   kd[1][gc * 128:(gc + 1) * 128, 0:INT])
                i2 = eng.dma_start(dst[:, INT:INT + LAT],
                                   kf[1][gc * 128:(gc + 1) * 128, 0:LAT])
                return [i1, i2]
            slot, buf, (lds, ldt), fr = wr.produce(
                "sync" if mode == "bf" else "gpsimd", dma_fn,
                free_amt=1 + spill_amt, n_dma=2, extra_waits=extra)
            if mode == "cast_spill":
                def fsp(eng, buf=buf, lds=lds, ldt=ldt, fr=fr, gc=gc,
                        kd=kd, kf=kf):
                    eng.wait_ge(lds, ldt)
                    eng.dma_start(kd[2][gc * 128:(gc + 1) * 128, 0:INT],
                                  buf[:, 0:INT]).then_inc(fr, 16)
                    eng.dma_start(kf[2][gc * 128:(gc + 1) * 128, 0:LAT],
                                  buf[:, INT:INT + LAT]).then_inc(fr, 16)
                p.add("sync", fsp)
                p.dma_marks.add(wr.fr_n[slot])

            def mk(eng, ps, buf=buf, lds=lds, ldt=ldt, dve_h=dve_h):
                eng.wait_ge(lds, ldt)
                eng.wait_ge(p.sems["DVE_H"], dve_h)
                for fc in range(IC):
                    eng.matmul(ps[:, :], buf[:, fc * 128:(fc + 1) * 128],
                               hx[:, fc * 512:(fc + 1) * 512],
                               start=(fc == 0), stop=False)
                for mc in range(LC):
                    i = eng.matmul(ps[:, :],
                                   buf[:, INT + mc * 128:INT + (mc + 1) * 128],
                                   moe[:, mc * 512:(mc + 1) * 512],
                                   start=False, stop=(mc == LC - 1))
                return i

            def ev(eng, ps, gc=gc):
                i = eng.tensor_copy(y1s[:, gc * 512:(gc + 1) * 512], ps[:, :])
                sl = y1s[:, gc * 512:(gc + 1) * 512]
                eng.tensor_mul(sqs[0][:, :], sl, sl).then_inc(p.sems["DVE_SBF"], 1)
                return i
            p.bump("DVE_SBF")
            pe_group(mk, "vector", ev, frees=[(fr, 1)])
            dve_sbf = p.val("DVE_SBF")

            def ft_stat(eng, gc=gc, dve_sbf=dve_sbf):
                eng.wait_ge(p.sems["DVE_SBF"], dve_sbf)
                eng.matmul(p4[0:1, :], ones_bf[:, 0:1],
                           y1s[:, gc * 512:(gc + 1) * 512],
                           start=(gc == 0), stop=(gc == HC - 1))
                eng.matmul(p5[0:1, :], ones_bf[:, 0:1], sqs[0][:, :],
                           start=(gc == 0), stop=(gc == HC - 1)).then_inc(
                               p.sems["PE_STATF"], 1)
            p.bump("PE_STATF")
            p.add("tensor", ft_stat)

        if env.get("stop_stage") == "down":
            continue
        pe_stat = p.val("PE_STATF")

        def fv_rows2(eng, pe_stat=pe_stat):
            eng.wait_ge(p.sems["PE_STATF"], pe_stat)
            eng.tensor_scalar(rows[0:1, 0:512], p4[0:1, :], 1.0 / HID, None,
                              op0=ALU.mult)
            eng.tensor_scalar(rows[0:1, 512:1024], p5[0:1, :], 1.0 / HID, None,
                              op0=ALU.mult)
            eng.tensor_mul(rows[0:1, 1024:1536], rows[0:1, 0:512], rows[0:1, 0:512])
            eng.tensor_sub(rows[0:1, 1024:1536], rows[0:1, 512:1024],
                           rows[0:1, 1024:1536]).then_inc(p.sems["DVE_ROWS"], 1)
        p.bump("DVE_ROWS")
        p.add("vector", fv_rows2)
        act_t2 = p.val("DVE_ROWS")

        def fs_rows2(eng, act_t2=act_t2):
            eng.wait_ge(p.sems["DVE_ROWS"], act_t2)
            eng.activation(rows[0:1, 1536:2048], rows[0:1, 1024:1536],
                           AF.Sqrt).then_inc(p.sems["ACT_ROWS"], 1)
        p.bump("ACT_ROWS")
        p.add("scalar", fs_rows2)
        act_v2 = p.val("ACT_ROWS")

        def fv_recip2(eng, act_v2=act_v2):
            eng.wait_ge(p.sems["ACT_ROWS"], act_v2)
            eng.reciprocal(rows[0:1, 512:1024], rows[0:1, 1536:2048]).then_inc(
                p.sems["DVE_ROWS"], 1)
        p.bump("DVE_ROWS")
        p.add("vector", fv_recip2)
        dve_rows2 = p.val("DVE_ROWS")
        bc_need2 = p.val("BC_FREE")

        def ft_bc2(eng, dve_rows2=dve_rows2, bc_need2=bc_need2):
            eng.wait_ge(p.sems["DVE_ROWS"], dve_rows2)
            if bc_need2:
                eng.wait_ge(p.sems["BC_FREE"], bc_need2)
            eng.matmul(p6[:, :], ones_row[0:1, :], rows[0:1, 0:512],
                       start=True, stop=True)
            eng.matmul(p7[:, :], ones_row[0:1, :], rows[0:1, 512:1024],
                       start=True, stop=True).then_inc(p.sems["PE_BC"], 1)
        pe_bc2 = p.bump("PE_BC")
        p.add("tensor", ft_bc2)

        for gc in range(HC):
            slot, buf, (lds, ldt), fr = xs.produce(
                "sync",
                lambda eng, dstb, gc=gc, t0=t0: eng.dma_start(
                    dstb[:, :], src[gc * 128:(gc + 1) * 128, t0:t0 + TA]),
                free_amt=1)
            st_slot = stg.i
            stg.i = (stg.i + 1) % stg.n
            sbuf = stg.slots[st_slot]
            need_free = stg.fr_expect[st_slot]
            last = (gc == HC - 1)

            def fv(eng, buf=buf, sbuf=sbuf, gc=gc, lds=lds, ldt=ldt, fr=fr,
                   pe_bc2=pe_bc2, need_free=need_free, st_slot=st_slot,
                   last=last):
                eng.wait_ge(lds, ldt)
                eng.wait_ge(p.sems["PE_BC"], pe_bc2)
                if need_free:
                    eng.wait_ge(stg.fr[st_slot], need_free)
                sl = y1s[:, gc * 512:(gc + 1) * 512]
                eng.tensor_sub(ztmp[:, :], sl, p6[:, :])
                eng.tensor_mul(ztmp[:, :], ztmp[:, :], p7[:, :])
                eng.tensor_scalar(ztmp[:, :], ztmp[:, :], gb2[:, gc:gc + 1],
                                  gb2[:, HC + gc:HC + gc + 1],
                                  op0=ALU.mult, op1=ALU.add)
                eng.tensor_add(sbuf[:, :], buf[:, :], ztmp[:, :]).then_inc(fr, 1)
                eng.engine_nop().then_inc(stg.ld["eng"][st_slot], 1)
                if last:
                    eng.engine_nop().then_inc(p.sems["BC_FREE"], 1)
            rdy = p.bump(stg.ld_n["eng"][st_slot], 1)
            p.add("vector", fv)
            if last:
                p.bump("BC_FREE")
            p.dma_marks.add(stg.fr_n[st_slot])
            stg.fr_expect[st_slot] = p.bump(stg.fr_n[st_slot], 16)

            def fst(eng, sbuf=sbuf, gc=gc, t0=t0, st_slot=st_slot, rdy=rdy):
                eng.wait_ge(stg.ld["eng"][st_slot], rdy)
                eng.dma_start(dst[gc * 128:(gc + 1) * 128, t0:t0 + TA],
                              sbuf[:, :]).then_inc(stg.fr[st_slot], 16)
            p.add("gpsimd", fst)

        if env.get("tile_barrier"):
            p.barrier()
        snap = env.get("snap")
        if snap is not None and t == 0:
            waits = [(stg.fr_n[i], p.val(stg.fr_n[i])) for i in range(stg.n)]

            def f_snap(eng, waits=tuple(waits)):
                for name, v in waits:
                    eng.wait_ge(p.sems[name], v)
                eng.dma_start(snap[:, :], dst[:, 0:TA]).then_inc(
                    p.sems["SNAP"], 16)
            p.dma_marks.add("SNAP")
            p.bump("SNAP", 16)
            p.add("sync", f_snap)

    _sync_end_wait(p, stg)



def plan_qkv_phase(p, env, src, wq, wk, wv, q_scr, K_sb, V_sb, bvb_sb):
    mp = env["main_ps"]
    y2 = env["y2"]
    stg = env["stg_ring"]
    main_rec = env.setdefault("main_rec", {})
    gidx = env.setdefault("gidx", [0])

    def pe_group(emit_mms, emit_evac, frees=()):
        g = gidx[0]
        gidx[0] += 1
        slot = g % 4
        ps = mp[:, slot * 512:(slot + 1) * 512]
        prev = main_rec.get(slot)
        pe_t = p.bump("PE_G")

        def ft(eng, prev=prev, ps=ps):
            if prev is not None:
                eng.wait_ge(p.sems[prev[0]], prev[1])
            emit_mms(eng, ps).then_inc(p.sems["PE_G"], 1)
        p.add("tensor", ft)
        ev_t = p.bump("DVE_EVAC")

        def fe(eng, pe_t=pe_t, ps=ps, frees=tuple(frees)):
            eng.wait_ge(p.sems["PE_G"], pe_t)
            emit_evac(eng, ps).then_inc(p.sems["DVE_EVAC"], 1)
            for sem, amt in frees:
                eng.nop().then_inc(sem, amt)
        p.add("vector", fe)
        main_rec[slot] = ("DVE_EVAC", ev_t)

    for tile in range(NTB):
        for tb in range(2):
            t0 = tile * TB + tb * 512
            plan_ln_pass(p, env, src, t0,
                         lambda hc, tb=tb: y2[:, hc * TB + tb * 512:
                                              hc * TB + (tb + 1) * 512])
        dve_y = p.val("DVE_Y")

        for fc in range(HC):
            buf, (lds, ldt), fr = plan_wcol_load(p, env, wq, fc * 128, HID, 2)
            for tb in range(2):
                def mk(eng, ps, buf=buf, lds=lds, ldt=ldt, tb=tb,
                       dve_y=dve_y):
                    eng.wait_ge(lds, ldt)
                    eng.wait_ge(p.sems["DVE_Y"], dve_y)
                    for hc in range(HC):
                        i = eng.matmul(
                            ps[:, :], buf[:, hc * 128:(hc + 1) * 128],
                            y2[:, hc * TB + tb * 512: hc * TB + (tb + 1) * 512],
                            start=(hc == 0), stop=(hc == HC - 1))
                    return i

                st_slot = stg.i
                stg.i = (stg.i + 1) % stg.n
                sbuf = stg.slots[st_slot]
                need_free = stg.fr_expect[st_slot]
                rdy = p.bump(stg.ld_n["eng"][st_slot], 1)
                p.dma_marks.add(stg.fr_n[st_slot])
                stg.fr_expect[st_slot] = p.bump(stg.fr_n[st_slot], 16)

                def ev(eng, ps, sbuf=sbuf, st_slot=st_slot, need_free=need_free):
                    if need_free:
                        eng.wait_ge(stg.fr[st_slot], need_free)
                    i = eng.tensor_scalar(sbuf[:, :], ps[:, :], QSCALE, None,
                                          op0=ALU.mult)
                    eng.engine_nop().then_inc(stg.ld["eng"][st_slot], 1)
                    return i
                pe_group(mk, ev, frees=[(fr, 1)])
                t0q = tile * TB + tb * 512

                def fst(eng, sbuf=sbuf, fc=fc, t0q=t0q, st_slot=st_slot, rdy=rdy):
                    eng.wait_ge(stg.ld["eng"][st_slot], rdy)
                    eng.dma_start(q_scr[fc * 128:(fc + 1) * 128, t0q:t0q + 512],
                                  sbuf[:, :]).then_inc(stg.fr[st_slot], 16)
                p.add("gpsimd", fst)

        for fc in range(NKV):
            buf, (lds, ldt), fr = plan_wcol_load(p, env, wk, fc * 128, HID, 2)
            for tb in range(2):
                def mk(eng, ps, buf=buf, lds=lds, ldt=ldt, tb=tb,
                       dve_y=dve_y):
                    eng.wait_ge(lds, ldt)
                    eng.wait_ge(p.sems["DVE_Y"], dve_y)
                    for hc in range(HC):
                        i = eng.matmul(
                            ps[:, :], buf[:, hc * 128:(hc + 1) * 128],
                            y2[:, hc * TB + tb * 512: hc * TB + (tb + 1) * 512],
                            start=(hc == 0), stop=(hc == HC - 1))
                    return i

                t0k = tile * TB + tb * 512

                def ev(eng, ps, fc=fc, t0k=t0k):
                    return eng.tensor_copy(
                        K_sb[:, fc * S + t0k: fc * S + t0k + 512], ps[:, :])
                pe_group(mk, ev, frees=[(fr, 1)])

        for dvq in range(4):
            buf, (lds, ldt), fr = plan_wcol_load(p, env, wv, dvq * 128, INT, 8)
            for tc in range(8):
                tglob = tile * 8 + tc
                last = (tc == 7)

                def mk(eng, ps, buf=buf, lds=lds, ldt=ldt, tc=tc,
                       dve_y=dve_y):
                    eng.wait_ge(lds, ldt)
                    eng.wait_ge(p.sems["DVE_Y"], dve_y)
                    for hc in range(HC):
                        i = eng.matmul(
                            ps[:, 0:256],
                            y2[:, hc * TB + tc * 128: hc * TB + (tc + 1) * 128],
                            buf[:, hc * 256:(hc + 1) * 256],
                            start=(hc == 0), stop=(hc == HC - 1))
                    return i

                def ev(eng, ps, tglob=tglob, dvq=dvq):
                    return eng.tensor_add(
                        V_sb[:, tglob * 1024 + dvq * 256:
                             tglob * 1024 + (dvq + 1) * 256],
                        bvb_sb[:, dvq * 256:(dvq + 1) * 256], ps[:, 0:256])
                pe_group(mk, ev, frees=[(fr, 1)])

    _sync_end_wait(p, stg)



def plan_attn_phase(p, env, q_scr, K_sb, V_sb, ctx_scr):
    mp = env["main_ps"]
    p4, p5, p6 = env["p4"], env["p5"], env["p6"]
    qh = env["qh_ring"]
    P_sb, PT_sb, ctx_sb = env["P_sb"], env["PT_sb"], env["ctx_sb"]
    id_bf = env["id_bf"]
    nm = env["nmax"]
    Dr = env["Dr"]
    sc0 = env["sc0"]
    recip = env["recip"]
    main_rec = env.setdefault("main_rec", {})
    pt_rec = {0: None, 1: None}
    ctx_rec = [None]
    psb_free = [None]

    PT3 = PT_sb.rearrange("p (j z) -> p j z", z=512)

    for g in range(4):
        if g > 0:
            st_done0 = p.val("CTX_ST")

            def fv_wait(eng, st_done0=st_done0):
                eng.wait_ge(p.sems["CTX_ST"], st_done0)
            p.add("vector", fv_wait)

        def qh_produce(hh, g=g):
            return qh.produce(
                "sync",
                lambda eng, dstb, hh=hh, g=g: eng.dma_start(
                    dstb[:, :], q_scr[hh * 128:(hh + 1) * 128,
                                      g * 512:(g + 1) * 512]),
                free_amt=1)

        qtoks = {0: qh_produce(0)}
        tp_meta = {}

        NBLK = NQ * 4
        for idx in range(NBLK + 1):
            if idx < NBLK:
                h, c = divmod(idx, 4)
                kv = h // 4
                par = idx % 2
                nmo = par * 8
                psb = P_sb[:, par * 2048:(par + 1) * 2048]
                slot, qbuf, (lds, ldt), qfr = qtoks[h]
                prevs = [main_rec.get(k) for k in range(4)]
                p.bump("PE_SC", 4)
                pe_sc = p.val("PE_SC")

                def ft(eng, qbuf=qbuf, c=c, kv=kv, lds=lds, ldt=ldt,
                       prevs=tuple(prevs)):
                    eng.wait_ge(lds, ldt)
                    for k in range(4):
                        pr = prevs[k]
                        if pr is not None:
                            eng.wait_ge(p.sems[pr[0]], pr[1])
                        eng.matmul(mp[:, k * 512:(k + 1) * 512],
                                   qbuf[:, c * 128:(c + 1) * 128],
                                   K_sb[:, kv * S + k * 512:
                                        kv * S + (k + 1) * 512],
                                   start=True, stop=True).then_inc(
                                       p.sems["PE_SC"], 1)
                p.add("tensor", ft)

                def fv_max01(eng, pe_sc=pe_sc, nmo=nmo):
                    for k in range(2):
                        eng.wait_ge(p.sems["PE_SC"], pe_sc - 3 + k)
                        eng.reduce_max(nm[:, nmo + k:nmo + k + 1],
                                       mp[:, k * 512:(k + 1) * 512],
                                       axis=AX.X, negate=True)
                    eng.tensor_tensor(nm[:, nmo + 4:nmo + 5],
                                      nm[:, nmo:nmo + 1],
                                      nm[:, nmo + 1:nmo + 2],
                                      op=ALU.min).then_inc(
                                          p.sems["DVE_MAX"], 1)
                dve_m01 = p.bump("DVE_MAX")
                p.add("vector", fv_max01)

                psbf = env.get("psb_free_%d" % par)

                def fs_exp0(eng, dve_m01=dve_m01, nmo=nmo, psbf=psbf, psb=psb,
                            dve_m23=dve_m01 + 1):
                    eng.wait_ge(p.sems["DVE_MAX"], dve_m23)
                    if psbf is not None:
                        eng.wait_ge(p.sems[psbf[0]], psbf[1])
                    eng.activation(psb[:, 0:1024], mp[:, 0:1024], AF.Exp,
                                   bias=nm[:, nmo + 5:nmo + 6], scale=1.0,
                                   accum_out=nm[:, nmo + 6:nmo + 7]).then_inc(
                                       p.sems["ACT_P"], 1)
                p.bump("ACT_P")
                act_p0 = p.val("ACT_P")
                p.add("scalar", fs_exp0)
                main_rec[0] = ("ACT_P", act_p0)
                main_rec[1] = ("ACT_P", act_p0)

                def fv_max23(eng, pe_sc=pe_sc, nmo=nmo, qfr=qfr,
                             last_c=(c == 3)):
                    for k in range(2, 4):
                        eng.wait_ge(p.sems["PE_SC"], pe_sc - 3 + k)
                        eng.reduce_max(nm[:, nmo + k:nmo + k + 1],
                                       mp[:, k * 512:(k + 1) * 512],
                                       axis=AX.X, negate=True)
                    eng.tensor_tensor(nm[:, nmo + 5:nmo + 6],
                                      nm[:, nmo + 2:nmo + 3],
                                      nm[:, nmo + 3:nmo + 4], op=ALU.min)
                    eng.tensor_tensor(nm[:, nmo + 5:nmo + 6],
                                      nm[:, nmo + 4:nmo + 5],
                                      nm[:, nmo + 5:nmo + 6], op=ALU.min)
                    eng.tensor_sub(nm[:, nmo + 7:nmo + 8],
                                   nm[:, nmo + 5:nmo + 6],
                                   nm[:, nmo + 4:nmo + 5]).then_inc(
                                       p.sems["DVE_MAX"], 1)
                    if last_c:
                        eng.engine_nop().then_inc(qfr, 1)
                dve_m23 = p.bump("DVE_MAX")
                p.add("vector", fv_max23)

                def fs_exp1(eng, dve_m23=dve_m23, nmo=nmo, psb=psb):
                    eng.wait_ge(p.sems["DVE_MAX"], dve_m23)
                    eng.activation(psb[:, 1024:2048], mp[:, 1024:2048], AF.Exp,
                                   bias=nm[:, nmo + 5:nmo + 6], scale=1.0,
                                   accum_out=nm[:, nmo + 7:nmo + 8]).then_inc(
                                       p.sems["ACT_P"], 1)
                p.bump("ACT_P")
                act_p1 = p.val("ACT_P")
                p.add("scalar", fs_exp1)
                main_rec[2] = ("ACT_P", act_p1)
                main_rec[3] = ("ACT_P", act_p1)

                r1 = p.bump("DVE_RCP")
                r2 = p.bump("DVE_RCP")

                def fv_rcp(eng, act_p1=act_p1, nmo=nmo, par=par, r1=r1, r2=r2,
                           psb=psb, psbf=psbf):
                    eng.wait_ge(p.sems["ACT_P"], act_p1)
                    if psbf is not None:
                        eng.wait_ge(p.sems[psbf[0]], psbf[1])
                    eng.tensor_add(recip[:, par:par + 1],
                                   nm[:, nmo + 6:nmo + 7],
                                   nm[:, nmo + 7:nmo + 8]).then_inc(
                                       p.sems["DVE_RCP"], 1)
                    eng.wait_ge(p.sems["DVE_RCP"], r1)
                    eng.reciprocal(recip[:, par:par + 1],
                                   recip[:, par:par + 1]).then_inc(
                                       p.sems["DVE_RCP"], 1)
                    eng.wait_ge(p.sems["DVE_RCP"], r2)
                    eng.tensor_scalar(Dr[:, par * 128:(par + 1) * 128],
                                      id_bf[:, :], recip[:, par:par + 1],
                                      None, op0=ALU.mult).then_inc(
                                          p.sems["GP_PN"], 1)
                gp_pn = p.bump("GP_PN")
                p.add("vector", fv_rcp)
                tp_meta[idx] = (par, gp_pn, psb)
                if c == 0 and h + 1 < NQ:
                    qtoks[h + 1] = qh_produce(h + 1)

            if idx >= 1:
                pidx = idx - 1
                h0, c0 = divmod(pidx, 4)
                par, gp_pn, psb = tp_meta.pop(pidx)
                for jq in range(4):
                    ptb = p4 if jq % 2 == 0 else p5
                    prev_pt = pt_rec[jq % 2]
                    p.bump("PE_PT2", 4)
                    pe_pt2 = p.val("PE_PT2")

                    def ft_tp(eng, jq=jq, gp_pn=gp_pn, prev_pt=prev_pt,
                              ptb=ptb, psb=psb, par=par):
                        eng.wait_ge(p.sems["GP_PN"], gp_pn)
                        if prev_pt is not None:
                            eng.wait_ge(p.sems[prev_pt[0]], prev_pt[1])
                        for jj in range(4):
                            j = jq * 4 + jj
                            eng.matmul(ptb[:, jj * 128:(jj + 1) * 128],
                                       psb[:, j * 128:(j + 1) * 128],
                                       Dr[:, par * 128:(par + 1) * 128],
                                       start=True, stop=True).then_inc(
                                           p.sems["PE_PT2"], 1)
                    p.add("tensor", ft_tp)
                    ev_eng = "vector" if jq % 2 == 0 else "scalar"
                    sem_name = {"vector": "DVE_EVAC",
                                "scalar": "ACT_EVAC"}[ev_eng]
                    ev_t = p.bump(sem_name)

                    def fe_tp(eng, jq=jq, c0=c0, pe_pt2=pe_pt2, ptb=ptb,
                              sem_name_=sem_name):
                        eng.wait_ge(p.sems["PE_PT2"], pe_pt2)
                        dst = PT3[:, jq * 4:(jq + 1) * 4,
                                  c0 * 128:(c0 + 1) * 128]
                        srcv = ptb.rearrange("p (j z) -> p j z", z=128)[:, :, :]
                        if hasattr(eng, "tensor_copy"):
                            eng.tensor_copy(dst, srcv).then_inc(
                                p.sems[sem_name_], 1)
                        else:
                            eng.copy(dst, srcv).then_inc(p.sems[sem_name_], 1)
                    p.add(ev_eng, fe_tp)
                    pt_rec[jq % 2] = (sem_name, ev_t)
                env["psb_free_%d" % par] = ("PE_PT2", p.val("PE_PT2"))

                if c0 == 3:
                    kv0 = h0 // 4
                    pt_done = [("DVE_EVAC", p.val("DVE_EVAC")),
                               ("ACT_EVAC", p.val("ACT_EVAC"))]
                    prev_ctx = ctx_rec[0]
                    pe_ctx2 = p.bump("PE_CTX2")

                    def ft_ctx(eng, kv0=kv0, pt_done=tuple(pt_done),
                               prev_ctx=prev_ctx):
                        for name, v in pt_done:
                            eng.wait_ge(p.sems[name], v)
                        if prev_ctx is not None:
                            eng.wait_ge(p.sems[prev_ctx[0]], prev_ctx[1])
                        for j in range(16):
                            i = eng.matmul(
                                p6[:, :],
                                V_sb[:, j * 1024 + kv0 * 128:
                                     j * 1024 + (kv0 + 1) * 128],
                                PT_sb[:, j * 512:(j + 1) * 512],
                                start=(j == 0), stop=(j == 15))
                        i.then_inc(p.sems["PE_CTX2"], 1)
                    p.add("tensor", ft_ctx)

                    def fv_ctx(eng, h0=h0, pe_ctx2=pe_ctx2):
                        eng.wait_ge(p.sems["PE_CTX2"], pe_ctx2)
                        eng.tensor_copy(ctx_sb[:, h0 * 512:(h0 + 1) * 512],
                                        p6[:, :]).then_inc(p.sems["DVE_CTX"], 1)
                    dve_ctx = p.bump("DVE_CTX")
                    p.add("vector", fv_ctx)
                    ctx_rec[0] = ("DVE_CTX", dve_ctx)

                    def fst(eng, h0=h0, g=g, dve_ctx=dve_ctx):
                        eng.wait_ge(p.sems["DVE_CTX"], dve_ctx)
                        eng.dma_start(
                            ctx_scr[h0 * 128:(h0 + 1) * 128,
                                    g * 512:(g + 1) * 512],
                            ctx_sb[:, h0 * 512:(h0 + 1) * 512]).then_inc(
                                p.sems["CTX_ST"], 16)
                    p.dma_marks.add("CTX_ST"); p.bump("CTX_ST", 16)
                    p.add("sync", fst)

    st_done = p.val("CTX_ST")

    def f_end(eng, st_done=st_done):
        eng.wait_ge(p.sems["CTX_ST"], st_done)
    p.add("sync", f_end)



def plan_wo_phase(p, env, wo, ctx_scr, x1_scr, x2_scr):
    mp = env["main_ps"]
    ctxt = env["ctxt"]
    xs, stg = env["xs_ring"], env["stg_ring"]
    main_rec = env.setdefault("main_rec", {})
    gidx = env.setdefault("gidx", [0])

    for tile in range(NTB):
        t0 = tile * TB
        for cc in range(HC):
            need = p.val("CTL_FREE")

            def fld(eng, cc=cc, t0=t0, need=need):
                if need:
                    eng.wait_ge(p.sems["CTL_FREE"], need)
                eng.dma_start(ctxt[:, cc * TB:(cc + 1) * TB],
                              ctx_scr[cc * 128:(cc + 1) * 128,
                                      t0:t0 + TB]).then_inc(p.sems["CTL_LD"], 16)
            p.dma_marks.add("CTL_LD"); p.bump("CTL_LD", 16)
            p.add("sync", fld)
        ld_all = p.val("CTL_LD")

        for fc in range(HC):
            buf, (lds, ldt), fr = plan_wcol_load(p, env, wo, fc * 128, HID, 2)
            for tb in range(2):
                g = gidx[0]
                gidx[0] += 1
                slot = g % 4
                ps = mp[:, slot * 512:(slot + 1) * 512]
                prev = main_rec.get(slot)
                pe_t = p.bump("PE_G")

                def mk(eng, buf=buf, lds=lds, ldt=ldt, tb=tb, prev=prev,
                       ld_all=ld_all, ps=ps):
                    eng.wait_ge(lds, ldt)
                    eng.wait_ge(p.sems["CTL_LD"], ld_all)
                    if prev is not None:
                        eng.wait_ge(p.sems[prev[0]], prev[1])
                    for cc in range(HC):
                        i = eng.matmul(
                            ps[:, :], buf[:, cc * 128:(cc + 1) * 128],
                            ctxt[:, cc * TB + tb * 512: cc * TB + (tb + 1) * 512],
                            start=(cc == 0), stop=(cc == HC - 1))
                    i.then_inc(p.sems["PE_G"], 1)
                p.add("tensor", mk)

                tt0 = t0 + tb * 512
                slot2, buf2, (lds2, ldt2), fr2 = xs.produce(
                    "sync",
                    lambda eng, dstb, fc=fc, tt0=tt0: eng.dma_start(
                        dstb[:, :], x1_scr[fc * 128:(fc + 1) * 128,
                                           tt0:tt0 + 512]),
                    free_amt=1)
                st_slot = stg.i
                stg.i = (stg.i + 1) % stg.n
                sbuf = stg.slots[st_slot]
                need_free = stg.fr_expect[st_slot]
                ev_t = p.bump("DVE_EVAC")
                rdy = p.bump(stg.ld_n["eng"][st_slot], 1)
                p.dma_marks.add(stg.fr_n[st_slot])
                stg.fr_expect[st_slot] = p.bump(stg.fr_n[st_slot], 16)

                def ev(eng, ps=ps, buf2=buf2, sbuf=sbuf, lds2=lds2, ldt2=ldt2,
                       fr=fr, fr2=fr2, pe_t=pe_t, need_free=need_free,
                       st_slot=st_slot):
                    eng.wait_ge(p.sems["PE_G"], pe_t)
                    eng.wait_ge(lds2, ldt2)
                    if need_free:
                        eng.wait_ge(stg.fr[st_slot], need_free)
                    i = eng.tensor_add(sbuf[:, :], buf2[:, :], ps[:, :])
                    i.then_inc(p.sems["DVE_EVAC"], 1)
                    eng.engine_nop().then_inc(fr, 1)
                    eng.engine_nop().then_inc(fr2, 1)
                    eng.engine_nop().then_inc(stg.ld["eng"][st_slot], 1)
                p.add("vector", ev)
                main_rec[slot] = ("DVE_EVAC", ev_t)

                def fst(eng, sbuf=sbuf, fc=fc, tt0=tt0, st_slot=st_slot, rdy=rdy):
                    eng.wait_ge(stg.ld["eng"][st_slot], rdy)
                    eng.dma_start(x2_scr[fc * 128:(fc + 1) * 128, tt0:tt0 + 512],
                                  sbuf[:, :]).then_inc(stg.fr[st_slot], 16)
                p.add("gpsimd", fst)

        done = p.val("PE_G")

        def f_free(eng, done=done):
            eng.wait_ge(p.sems["PE_G"], done)
            eng.nop().then_inc(p.sems["CTL_FREE"], 1)
        p.bump("CTL_FREE")
        p.add("tensor", f_free)

    _sync_end_wait(p, stg)



def build_program(dbg=(), phases=("A", "B", "C", "D1", "D2")):
    nc = bass.Bass("TRN2", target_bir_lowering=False, debug=False,
                   detect_race_conditions=False)

    def dram_in(name, shape, dt):
        return nc.dram_tensor(name, shape, dt, kind="ExternalInput").ap()

    def dram_scr(name, shape, dt):
        kind = "ExternalOutput" if name in dbg else "Internal"
        return nc.dram_tensor(name, shape, dt, kind=kind).ap()

    x0 = dram_in("x0", [HID, S], F32)
    wq_t = dram_in("wq_t", [HID, HID], F32)
    wk_t = dram_in("wk_t", [NKV * 128, HID], F32)
    wv_t = dram_in("wv_t", [512, INT], F32)
    wo_t = dram_in("wo_t", [HID, HID], F32)
    up_t = dram_in("up_t", [INT, HID], F32)
    down_t = dram_in("down_t", [HID, INT], F32)
    fc1_t = dram_in("fc1_t", [LAT, HID], F32)
    fc2_t = dram_in("fc2_t", [HID, LAT], F32)
    exp_t = dram_in("exp_t", [LAT, LAT], F32)
    gate_t = dram_in("gate_t", [128, 256], F32)
    gb_in = dram_in("gb_in", [128, 64], F32)
    bv_in = dram_in("bv_in", [1, 1024], F32)
    idbf_in = dram_in("idbf_in", [128, 128], BF16)
    idf_in = dram_in("idf_in", [128, 128], F32)
    p28_in = dram_in("p28_in", [8, 28], BF16)

    out = nc.dram_tensor("out", [HID, S], F32, kind="ExternalOutput").ap()

    x1_scr = dram_scr("x1_scr", [HID, S], F32)
    x2_scr = dram_scr("x2_scr", [HID, S], F32)
    q_scr = dram_scr("q_scr", [HID, S], BF16)
    ctx_scr = dram_scr("ctx_scr", [HID, S], BF16)
    up_bf = dram_scr("up_bf", [INT, HID], BF16)
    down_bf = dram_scr("down_bf", [HID, INT], BF16)
    fc1_bf = dram_scr("fc1_bf", [LAT, HID], BF16)
    fc2_bf = dram_scr("fc2_bf", [HID, LAT], BF16)
    exp_bf = dram_scr("exp_bf", [LAT, LAT], BF16)

    with contextlib.ExitStack() as master:
        p = Plan(nc, master)
        for s in SEM_NAMES:
            p.sem(s)

        sb = lambda name, shape, dt: master.enter_context(
            nc.sbuf_tensor(name, shape, dt))
        gate_sb = sb("gate_sb", [128, 256], BF16)
        gb2 = sb("gb2", [128, 64], F32)
        bvb_sb = sb("bvb_sb", [128, 1024], F32)
        id_bf = sb("id_bf", [128, 128], BF16)
        id_f32 = sb("id_f32", [128, 128], F32)
        p28_sb = sb("p28_sb", [8, 28], BF16)
        ones_bf = sb("ones_bf", [128, 1], BF16)
        ones_row = sb("ones_row", [1, 128], F32)
        rows_sb = sb("rows_sb", [1, 2048], F32)
        fence_sb = sb("fence_sb", [1, 8], F32)
        p.fence_sb = fence_sb

        xs_glob = Ring(p, "GX", [sb(f"gxs{i}", [128, 512], F32)
                                 for i in range(6)], queues=("sync",))
        stg_glob = Ring(p, "GS", [sb(f"gst{i}", [128, 512], F32)
                                  for i in range(4)], queues=("eng",))
        main_ps = master.enter_context(nc.psum_tensor("main_ps", [128, 2048], F32))
        p4 = master.enter_context(nc.psum_tensor("p4", [128, 512], F32))
        p5 = master.enter_context(nc.psum_tensor("p5", [128, 512], F32))
        p6 = master.enter_context(nc.psum_tensor("p6", [128, 512], F32))
        p7 = master.enter_context(nc.psum_tensor("p7", [128, 512], F32))

        base_env = dict(main_ps=main_ps, p4=p4, p5=p5, p6=p6, p7=p7,
                        ones_bf=ones_bf, ones_row=ones_row, rows_sb=rows_sb,
                        gate_sb=gate_sb, gb2=gb2, id_bf=id_bf, id_f32=id_f32,
                        p28_sb=p28_sb)

        def f_p0_g(eng):
            eng.dma_start(gate_sb[:, :], gate_t[:, :]).then_inc(p.sems["P0G"], 16)
        p.dma_marks.add("P0G"); p0g = p.bump("P0G", 16)
        p.add("gpsimd", f_p0_g)

        def f_p0_s(eng):
            eng.dma_start(gb2[:, :], gb_in[:, :]).then_inc(p.sems["P0"], 16)
            eng.dma_start(id_bf[:, :], idbf_in[:, :]).then_inc(p.sems["P0"], 16)
            eng.dma_start(id_f32[:, :], idf_in[:, :]).then_inc(p.sems["P0"], 16)
            eng.dma_start(p28_sb[:, :], p28_in[:, :]).then_inc(p.sems["P0"], 16)
            eng.dma_start(rows_sb[0:1, 0:1024], bv_in[0:1, :]).then_inc(
                p.sems["P0"], 16)
        p.bump("P0", 16 * 5)
        p0_dma = p.val("P0")
        p.add("sync", f_p0_s)

        def f_p0_v(eng, p0_dma=p0_dma, p0g=p0g):
            eng.wait_ge(p.sems["P0"], p0_dma)
            eng.wait_ge(p.sems["P0G"], p0g)
            eng.memset(fence_sb[0:1, :], 1.0)
            eng.memset(ones_bf[:, :], 1.0)
            eng.memset(ones_row[0:1, :], 1.0)
            eng.tensor_scalar(gb2[:, :], gb2[:, :], 0.5, None,
                              op0=ALU.mult).then_inc(p.sems["DVE_ROWS"], 1)
        dve_p0 = p.bump("DVE_ROWS")
        p.add("vector", f_p0_v)

        def f_p0_t(eng, dve_p0=dve_p0):
            eng.wait_ge(p.sems["DVE_ROWS"], dve_p0)
            eng.matmul(p6[:, :], ones_row[0:1, :], rows_sb[0:1, 0:512],
                       start=True, stop=True)
            eng.matmul(p7[:, :], ones_row[0:1, :], rows_sb[0:1, 512:1024],
                       start=True, stop=True).then_inc(p.sems["PE_BC"], 1)
        pe_p0 = p.bump("PE_BC")
        p.add("tensor", f_p0_t)

        def f_p0_v2(eng, pe_p0=pe_p0):
            eng.wait_ge(p.sems["PE_BC"], pe_p0)
            eng.tensor_copy(bvb_sb[:, 0:512], p6[:, :])
            eng.tensor_copy(bvb_sb[:, 512:1024], p7[:, :]).then_inc(
                p.sems["BC_FREE"], 1)
        p.bump("BC_FREE")
        p.add("vector", f_p0_v2)
        p.barrier()
        p.flush()

        if "A" in phases:
          with contextlib.ExitStack() as sa:
            sbA = lambda name, shape, dt: sa.enter_context(
                nc.sbuf_tensor(name, shape, dt))
            envA = dict(base_env)
            envA["y1s"] = sbA("y1s", [128, 16384], BF16)
            envA["hx"] = sbA("hx", [128, 32768], BF16)
            envA["moe"] = sbA("moe", [128, 8192], BF16)
            awpool = sbA("awpool", [128, 20480], BF16)
            envA["ring_sm"] = Ring(
                p, "AWS", [awpool[:, i * 4096:(i + 1) * 4096] for i in range(5)],
                queues=("gpsimd", "sync"))
            envA["ring_dn"] = Ring(
                p, "AWD", [awpool[:, j * 10240:(j + 1) * 10240] for j in range(2)],
                queues=("gpsimd", "sync"))
            envA["xs_ring"] = xs_glob
            envA["stg_ring"] = stg_glob
            envA["xbf"] = [sbA("axbf0", [128, 512], BF16),
                           sbA("axbf1", [128, 512], BF16)]
            envA["sqs"] = [sbA("asqs0", [128, 512], BF16),
                          sbA("asqs1", [128, 512], BF16)]
            envA["ztmp"] = sbA("aztmp", [128, 512], F32)
            envA["logits_bf"] = sbA("alogits", [8, 512], BF16)
            envA["pairs_bf"] = sbA("apairs", [28, 512], BF16)
            envA["rwcols"] = sbA("arwcols", [128, 4], F32)
            envA["rwrow_sb"] = sbA("arwrow", [1, 512], F32)
            envA["rwb_sb"] = sbA("arwb", [128, 512], F32)

            envA["snap"] = dram_scr("snap0", [HID, TA], F32) if "snap0" in dbg else None

            def wkeysA(t):
                if t == 0:
                    return {"up": ("cast_spill", up_t, up_bf),
                            "down": ("cast_spill", down_t, down_bf),
                            "fc1": ("cast_spill", fc1_t, fc1_bf),
                            "fc2": ("cast_spill", fc2_t, fc2_bf),
                            "expert": ("cast_spill", exp_t, exp_bf)}
                return {"up": ("bf", up_bf), "down": ("bf", down_bf),
                        "fc1": ("bf", fc1_bf), "fc2": ("bf", fc2_bf),
                        "expert": ("bf", exp_bf)}
            plan_ffn_phase(p, envA, x0, x1_scr, wkeysA)
            p.barrier()
            p.flush()

        if "B" in phases or "C" in phases:
          with contextlib.ExitStack() as sbc:
            sbBC = lambda name, shape, dt: sbc.enter_context(
                nc.sbuf_tensor(name, shape, dt))
            K_sb = sbBC("K_sb", [128, NKV * S], BF16)
            V_sb = sbBC("V_sb", [128, 16 * 1024], BF16)

            if "B" in phases:
              with contextlib.ExitStack() as sB:
                sbB = lambda name, shape, dt: sB.enter_context(
                    nc.sbuf_tensor(name, shape, dt))
                envB = dict(base_env)
                envB["y2"] = sbB("y2", [128, HC * TB], BF16)
                envB["w_ring"] = Ring(p, "BW", [sbB("bwb0", [128, 8192], BF16),
                                                sbB("bwb1", [128, 8192], BF16)],
                                      queues=("gpsimd",))
                envB["xs_ring"] = xs_glob
                envB["stg_ring"] = Ring(p, "BS", [sbB("bst0", [128, 512], BF16),
                                                  sbB("bst1", [128, 512], BF16)],
                                        queues=("eng",))
                envB["xbf"] = [sbB("bxbf0", [128, 512], BF16),
                               sbB("bxbf1", [128, 512], BF16)]
                envB["sqs"] = [sbB("bsqs0", [128, 512], BF16),
                              sbB("bsqs1", [128, 512], BF16)]
                plan_qkv_phase(p, envB, x1_scr, ("cast", wq_t), ("cast", wk_t),
                               ("cast", wv_t), q_scr, K_sb, V_sb, bvb_sb)
                p.barrier()
                p.flush()

            if "kv_dbg" in dbg:
                k_dbg = nc.dram_tensor("k_dbg", [128, NKV * S], F32,
                                       kind="ExternalOutput").ap()
                v_dbg = nc.dram_tensor("v_dbg", [128, 16 * 1024], F32,
                                       kind="ExternalOutput").ap()
                q_dbg = nc.dram_tensor("q_dbg", [HID, S], F32,
                                       kind="ExternalOutput").ap()

                def f_kv(eng):
                    eng.dma_start(k_dbg[:, :], K_sb[:, :]).then_inc(
                        p.sems["P0G"], 16)
                    eng.dma_start(v_dbg[:, :], V_sb[:, :]).then_inc(
                        p.sems["P0G"], 16)
                    for hc in range(HC):
                        eng.dma_start(q_dbg[hc * 128:(hc + 1) * 128, :],
                                      q_scr[hc * 128:(hc + 1) * 128, :]).then_inc(
                                          p.sems["P0G"], 16)
                p.bump("P0G", 16 * (2 + HC))
                p.add("gpsimd", f_kv)
                p.barrier()
                p.flush()

            if "C" in phases:
              with contextlib.ExitStack() as sC:
                sbC = lambda name, shape, dt: sC.enter_context(
                    nc.sbuf_tensor(name, shape, dt))
                envC = dict(base_env)
                envC["qh_ring"] = Ring(p, "CQ", [sbC("cqh0", [128, 512], BF16),
                                                 sbC("cqh1", [128, 512], BF16)])
                envC["P_sb"] = sbC("cP", [128, 4096], BF16)
                envC["PT_sb"] = sbC("cPT", [128, 8192], BF16)
                envC["ctx_sb"] = sbC("cctx", [128, 16384], BF16)
                envC["nmax"] = sbC("cnmax", [128, 16], F32)
                envC["Dr"] = sbC("cDr", [128, 256], BF16)
                envC["sc0"] = sbC("csc0", [128, 2], F32)
                envC["recip"] = sbC("crecip", [128, 2], F32)
                if "attn_dbg" in dbg:
                    envC["nm_dbg"] = nc.dram_tensor(
                        "nm_dbg", [128, 18], F32, kind="ExternalOutput").ap()
                    envC["attn_dbg"] = (
                        nc.dram_tensor("p_dbg", [128, 2048], F32,
                                       kind="ExternalOutput").ap(),
                        nc.dram_tensor("pt_dbg", [128, 8192], F32,
                                       kind="ExternalOutput").ap())
                plan_attn_phase(p, envC, q_scr, K_sb, V_sb, ctx_scr)
                p.barrier()
                p.flush()

        if "ctx_dbg" in dbg:
            ctx_dbg = nc.dram_tensor("ctx_dbg", [HID, S], F32,
                                     kind="ExternalOutput").ap()

            def f_cd(eng):
                for hc in range(HC):
                    eng.dma_start(ctx_dbg[hc * 128:(hc + 1) * 128, :],
                                  ctx_scr[hc * 128:(hc + 1) * 128, :]).then_inc(
                                      p.sems["P0G"], 16)
            p.bump("P0G", 16 * HC)
            p.add("gpsimd", f_cd)
            p.barrier()
            p.flush()

        if "D1" in phases:
          with contextlib.ExitStack() as sD:
            sbD = lambda name, shape, dt: sD.enter_context(
                nc.sbuf_tensor(name, shape, dt))
            envD = dict(base_env)
            envD["ctxt"] = sbD("dctxt", [128, HC * TB], BF16)
            envD["w_ring"] = Ring(p, "DW", [sbD("dwb0", [128, 8192], BF16),
                                            sbD("dwb1", [128, 8192], BF16)],
                                  queues=("gpsimd",))
            envD["xs_ring"] = xs_glob
            envD["stg_ring"] = stg_glob
            plan_wo_phase(p, envD, ("cast", wo_t), ctx_scr, x1_scr, x2_scr)
            p.barrier()
            p.flush()

        if "D2" in phases:
          with contextlib.ExitStack() as sE:
            sbE = lambda name, shape, dt: sE.enter_context(
                nc.sbuf_tensor(name, shape, dt))
            envE = dict(base_env)
            envE["y1s"] = sbE("ey1s", [128, 16384], BF16)
            envE["hx"] = sbE("ehx", [128, 32768], BF16)
            envE["moe"] = sbE("emoe", [128, 8192], BF16)
            ewpool = sbE("ewpool", [128, 20480], BF16)
            envE["ring_sm"] = Ring(
                p, "EWS", [ewpool[:, i * 4096:(i + 1) * 4096] for i in range(5)],
                queues=("sync",))
            envE["ring_dn"] = Ring(
                p, "EWD", [ewpool[:, j * 10240:(j + 1) * 10240] for j in range(2)],
                queues=("sync",))
            envE["xs_ring"] = xs_glob
            envE["stg_ring"] = stg_glob
            envE["xbf"] = [sbE("exbf0", [128, 512], BF16),
                           sbE("exbf1", [128, 512], BF16)]
            envE["sqs"] = [sbE("esqs0", [128, 512], BF16),
                          sbE("esqs1", [128, 512], BF16)]
            envE["ztmp"] = sbE("eztmp", [128, 512], F32)
            envE["logits_bf"] = sbE("elogits", [8, 512], BF16)
            envE["pairs_bf"] = sbE("epairs", [28, 512], BF16)
            envE["rwcols"] = sbE("erwcols", [128, 4], F32)
            envE["rwrow_sb"] = sbE("erwrow", [1, 512], F32)
            envE["rwb_sb"] = sbE("erwb", [128, 512], F32)

            def wkeysE(t):
                return {"up": ("bf", up_bf), "down": ("bf", down_bf),
                        "fc1": ("bf", fc1_bf), "fc2": ("bf", fc2_bf),
                        "expert": ("bf", exp_bf)}
            plan_ffn_phase(p, envE, x2_scr, out, wkeysE)
            p.barrier()
            p.flush()

    return nc



def _prep_inputs(kw):
    bf = ml_dtypes.bfloat16
    w = {}
    w["wq_t"] = tile_w(np.ascontiguousarray(kw["L0_attn_Wq"].T))
    w["wk_t"] = tile_w(np.ascontiguousarray(kw["L0_attn_Wk"].T))
    w["wv_t"] = tile_wv(np.ascontiguousarray(kw["L0_attn_Wv"].T))
    w["wo_t"] = tile_w(np.ascontiguousarray(kw["L0_attn_Wo"].T))
    w["up_t"] = tile_w(kw["L0_ffn_up_w"])
    w["down_t"] = tile_w(kw["L0_ffn_down_w"])
    w["fc1_t"] = tile_w(kw["L0_ffn_fc1_w"])
    w["fc2_t"] = tile_w(kw["L0_ffn_fc2_w"])
    w["exp_t"] = tile_w(kw["L0_ffn_expert_w"])
    w["gate_t"] = np.ascontiguousarray(
        kw["L0_ffn_gate_w"].reshape(32, 128, 8).transpose(1, 0, 2).reshape(128, 256))
    gb = np.concatenate([kw["L0_ffn_ln_gamma"].reshape(32, 128).T,
                         kw["L0_ffn_ln_beta"].reshape(32, 128).T], axis=1)
    w["gb_in"] = np.ascontiguousarray(gb.astype(np.float32))
    w["bv_in"] = np.ascontiguousarray(
        kw["L0_attn_bv"].reshape(1, 1024).astype(np.float32))
    w["idbf_in"] = np.eye(128, dtype=bf)
    w["idf_in"] = np.eye(128, dtype=np.float32)
    w["p28_in"] = _pairs_mat().astype(bf)
    for k in list(w.keys()):
        if w[k].dtype == np.float64:
            w[k] = w[k].astype(np.float32)
    return w


def kernel(**kw):
    x = np.asarray(kw["L0_attn_hidden_states"], dtype=np.float32)
    w = _prep_inputs({k: np.asarray(v) for k, v in kw.items()
                      if k != "L0_attn_hidden_states"})
    nc = build_program()
    in_maps = []
    for b in range(8):
        m = dict(w)
        m["x0"] = np.ascontiguousarray(x[b].T)
        in_maps.append(m)
    res = run_bass_kernel_spmd(nc, in_maps, core_ids=list(range(8)))
    outs = [np.ascontiguousarray(r["out"].T) for r in res.results]
    return np.stack(outs, axis=0).astype(np.float32)
